# version 25
# baseline (speedup 1.0000x reference)
import numpy as np
import ml_dtypes
from contextlib import ExitStack

import concourse.bass as bass
import concourse.mybir as mybir
from concourse.bass_utils import run_bass_kernel_spmd

F32 = mybir.dt.float32
BF16 = mybir.dt.bfloat16
AF = mybir.ActivationFunctionType
ALU = mybir.AluOpType
AXX = mybir.AxisListType.X
BF = ml_dtypes.bfloat16

B, S, D, H, DK = 4, 4096, 1024, 16, 64
NG, GT = 8, 512
NT = S // 128
NHP = 4

UNITS = [(g, hp, kt) for g in range(NG) for hp in range(NHP) for kt in range(4 * g + 4)]
NU = len(UNITS)
ULAST = {}
for i, (g, hp, kt) in enumerate(UNITS):
    ULAST[(g, hp)] = i
UOFF = [0] * (NG + 1)
for g in range(NG):
    UOFF[g + 1] = UOFF[g] + 4 * (4 * g + 4)

def _sched(g):
    n = 16 * (g + 1)
    s = max(17, n - 40)
    P = {}
    lpos = {"lna": (0, 1, 10, 11), "r0": (1, 2, 11, 12), "lnb": (2, 3, 12, 13),
            "tma": (9, 10, 13, 14), "tmb": (9, 10, 13, 14),
            "eva": (10, 11, 14, 15), "evb": (10, 11, 14, 15)}
    for k, ps in lpos.items():
        for j in range(4):
            P[(k, j)] = ps[j]
    for hp in range(4):
        b = s + 8 * hp
        for k, off in (("qmm", 0), ("kmm", 1), ("tq", 1), ("mq", 2), ("tk", 2),
                       ("mk", 3), ("cmbq", 3), ("cmbk", 4), ("qhe", 4), ("khe", 5),
                       ("sqq", 5), ("sqk", 6), ("nq", 6), ("nk", 7), ("rmq", 7), ("rmk", 8)):
            P[(k, hp)] = b + off
    P[("b2", 0)] = s + 33
    P[("bexp", 0)] = s + 34
    P[("bfin", 0)] = s + 35
    for j, p in ((0, s + 32), (1, s + 36), (2, s + 37), (3, s + 38)):
        P[("v", j)] = p
    for j, p in ((0, s + 33), (1, s + 37), (2, s + 38), (3, s + 39)):
        P[("vev", j)] = p
    return {k: min(p, n) for k, p in P.items()}


ACT_TR = False
def _chain_order(stages):
    pos = _sched(0)
    n = 16

    def key_of(t):
        i, (key, _fn) = t
        if key[0] == "v":
            return (15, 100 + key[1])
        if key[0] == "vev":
            return (15, 1000 + key[1])
        return (min(pos[key], n), i)

    return [fn for _i, (key, fn) in sorted(enumerate(stages), key=key_of)]


NWC = 28
NCONST = 11

_DRAIN_OPS = {
    "tensor_copy", "tensor_tensor", "tensor_add", "tensor_sub", "tensor_mul",
    "tensor_max", "tensor_scalar", "tensor_single_scalar", "tensor_scalar_add",
    "tensor_scalar_sub", "tensor_scalar_mul", "tensor_scalar_max", "tensor_scalar_min",
    "tensor_reduce", "reduce_max", "reduce_sum", "scalar_tensor_tensor",
    "bn_stats", "bn_aggr", "reciprocal", "memset", "memzero", "activation",
    "add", "mul", "copy", "square", "sqrt", "select", "transpose",
}
_DRAINS_ON = [True]


def _ap_tensor_names(args):
    names = []
    for a in args:
        t = getattr(a, "tensor", None)
        if t is not None:
            names.append(getattr(t, "name", str(t)))
    return names


class _Draining:

    def __init__(self, eng):
        self._e = eng
        self._written = set()
        self._read = set()

    def __getattr__(self, name):
        attr = getattr(self._e, name)
        if name in _DRAIN_OPS and _DRAINS_ON[0]:
            def wrapped(*a, **k):
                names = _ap_tensor_names(list(a) + list(k.values()))
                out = names[0] if names else None
                need = any(n in self._written for n in names) or (out in self._read)
                if need:
                    self._e.drain()
                    self._written.clear()
                    self._read.clear()
                if out is not None:
                    self._written.add(out)
                for n in names[1:]:
                    self._read.add(n)
                return attr(*a, **k)
            return wrapped
        return attr


def _build(drains=True):
    _DRAINS_ON[0] = drains
    nc = bass.Bass("TRN2", target_bir_lowering=False, debug=False)

    def din(name, shape, dt):
        return nc.dram_tensor(name, shape, dt, kind="ExternalInput").ap()

    x_b = din("x_b", [S, D], F32)
    x_own = din("x_own", [S // 2, D], F32)
    wq_d = din("wq_d", [8, 128, 512], F32)
    wk_d = din("wk_d", [8, 128, 512], F32)
    wv_d = din("wv_d", [8, 128, 512], F32)
    wo_d = din("wo_d", [4, 128, 1024], F32)
    bq_d = din("bq_d", [128, 4], F32)
    bk_d = din("bk_d", [128, 4], F32)
    bv_d = din("bv_d", [128, 512], F32)
    t1_d = din("t1_d", [128, S], F32)
    t2_d = din("t2_d", [128, S], F32)
    msk_d = din("msk_d", [128, 128], BF16)
    idn_d = din("idn_d", [128, 128], BF16)
    oblk_d = din("oblk_d", [128, 64], BF16)
    g_d = din("g_d", [128, D], F32)
    bb_d = din("bb_d", [128, D], F32)
    bo_d = din("bo_d", [128, D], F32)
    a1_d = din("a1_d", [128, 128], BF16)
    a2_d = din("a2_d", [128, 128], BF16)
    out_d = nc.dram_tensor("out", [S // 2, D], F32, kind="ExternalOutput").ap()
    cc_in = nc.dram_tensor("cc_in", [S, D], F32)
    cc_out = nc.dram_tensor("cc_out", [S // 2, D], F32)
    wqh_d = nc.dram_tensor("wqh_d", [4, 8, 128, 128], BF16)
    wkh_d = nc.dram_tensor("wkh_d", [4, 8, 128, 128], BF16)

    ctx = ExitStack()
    sem = {}
    for name in ["d_ws0", "d_ws1", "d_ws2", "d_ws3", "d_ws4", "d_ws5", "d_ws6", "d_ws7",
                 "d_wc", "d_wb0", "d_wb1", "s_trm", "s_utg", "d_x0", "d_x1", "d_t0", "d_t1",
                 "d_tr0", "d_tr1", "d_wq0", "d_wq1",
                 "d_pa0", "d_pa1", "d_a3", "d_out",
                 "s_cast", "s_w", "s_r0", "s_ln",
                 "s_qmm", "s_kmm", "s_qe", "s_ke",
                 "s_mq", "s_mk", "s_cq", "s_ck", "s_qhe", "s_khe",
                 "s_rq", "s_rk", "s_nmm", "s_nmax", "s_vmm", "s_vev",
                 "s_b2", "s_bexp", "s_bias",
                 "s_exp", "s_qk2", "s_pv", "s_rr", "s_onorm",
                 "s_womm", "s_paev", "s_cc",
                 "s_t3", "s_x13", "s_w3", "s_r3", "s_z3", "s_zg3", "s_o3"]:
        sem[name] = ctx.enter_context(nc.semaphore(name))

    gsb = lambda n, sh, dt: ctx.enter_context(nc.sbuf_tensor(n, sh, dt))
    idn_sb = gsb("idn_sb", [128, 128], BF16)
    oblk_sb = gsb("oblk_sb", [128, 64], BF16)
    a1_sb = gsb("a1_sb", [128, 128], BF16)
    a2_sb = gsb("a2_sb", [128, 128], BF16)
    bias_sb = gsb("bias_sb", [128, 2, 4], F32)
    nmaxq_sb = gsb("nmaxq_sb", [64, 4], F32)
    nmaxk_sb = gsb("nmaxk_sb", [64, 4], F32)
    nk2_sb = gsb("nk2_sb", [64, 4], F32)
    bsc_sb = gsb("bsc_sb", [64, 12], F32)
    bq_sb = gsb("bq_sb", [128, 4], F32)
    bk_sb = gsb("bk_sb", [128, 4], F32)
    bv_sb = gsb("bv_sb", [128, 512], F32)
    khat = gsb("khat", [128, 4, S], BF16)
    qhat = gsb("qhat", [128, 2, 4, GT], BF16)
    vau = gsb("vau", [128, 4, NT, 192], BF16)
    onog = gsb("onog", [128, 4, GT], BF16)
    wo_bf = gsb("wo_bf", [128, 4, 1024], BF16)
    pa_sb = gsb("pa_sb", [128, 2, 1024], F32)
    stg_sb = pa_sb[:, 0, :]
    p_sb = [gsb("p0_sb", [128, 1024], BF16), gsb("p1_sb", [128, 1024], BF16),
            gsb("p2_sb", [128, 1024], BF16)]
    msk_sb = gsb("msk_sb", [128, 128], BF16)
    rr_sb = gsb("rr_sb", [32, 2, GT], F32)
    wvr_sb = gsb("wvr_sb", [128, 8, 512], BF16)
    wqr_sb = gsb("wqr_sb", [128, 2, 8, 128], BF16)
    wkr_sb = gsb("wkr_sb", [128, 2, 8, 128], BF16)
    t1_sb = gsb("t1_sb", [128, 2, GT], F32)
    t2_sb = gsb("t2_sb", [128, 2, GT], F32)
    x_sb = gsb("x_sb", [128, 2, 1024], F32)
    ubf_sb = gsb("ubf_sb", [128, 2, 1024], BF16)
    utg_sb = gsb("utg_sb", [128, 8, GT], BF16)
    tq_sb = gsb("tq_sb", [128, GT], F32)
    tk_sb = gsb("tk_sb", [128, GT], F32)
    m1q_sb = gsb("m1q_sb", [128, GT], BF16)
    m2q_sb = gsb("m2q_sb", [128, GT], BF16)
    m1k_sb = gsb("m1k_sb", [128, GT], BF16)
    m2k_sb = gsb("m2k_sb", [128, GT], BF16)
    sqq_sb = gsb("sqq_sb", [128, GT], BF16)
    sqk_sb = gsb("sqk_sb", [128, GT], BF16)
    st_sb = gsb("st_sb", [128, 12], F32)
    ag_sb = gsb("ag_sb", [128, 4], F32)
    sc_sb = gsb("sc_sb", [128, 16], F32)
    wln_sb = gsb("wln_sb", [128, 4], F32)
    r0_sb = gsb("r0_sb", [128, 4], F32)
    g_sb = gsb("g_sb", [128, D], F32)
    bbc_sb = gsb("bbc_sb", [128, D], F32)
    bo_sb = gsb("bo_sb", [128, D], F32)
    x3_sb = gsb("x3_sb", [128, 1024], F32)
    t3_sb = gsb("t3_sb", [128, 1024], F32)
    x13_sb = gsb("x13_sb", [128, 2, 1024], F32)
    zg3_sb = gsb("zg3_sb", [128, 1024], F32)
    st3_sb = gsb("st3_sb", [128, 12], F32)
    ag3_sb = gsb("ag3_sb", [128, 4], F32)
    sc3_sb = gsb("sc3_sb", [128, 16], F32)
    m3_sb = gsb("m3_sb", [128, 2], F32)
    w3_sb = gsb("w3_sb", [128, 2], F32)
    r3_sb = gsb("r3_sb", [128, 2], F32)

    s_pair = [ctx.enter_context(nc.psum_tensor(f"s_pair{i}", [128, 1024], F32)) for i in range(2)]
    o_psA = ctx.enter_context(nc.psum_tensor("o_psA", [128, 512], F32))
    o_psB = ctx.enter_context(nc.psum_tensor("o_psB", [128, 512], F32))
    p1a_ps = ctx.enter_context(nc.psum_tensor("p1a_ps", [128, 512], F32))
    p1b_ps = ctx.enter_context(nc.psum_tensor("p1b_ps", [128, 512], F32))

    block = ctx.enter_context(nc.Block())
    CHUNKS = ([("wq", f) for f in range(8)] + [("wk", f) for f in range(8)]
              + [("wv", f) for f in range(8)] + [("wo", f) for f in range(4)])

    @block.sync
    def _(sync):

        def xdma(tt):
            if tt >= 2:
                sync.wait_ge(sem["s_ln"], tt - 1)
            sync.dma_start(out=x_sb[:, tt % 2, :],
                           in_=x_b[tt * 128:(tt + 1) * 128, :]).then_inc(sem["d_x%d" % (tt % 2)], 16)

        def prep_lite(gp):
            xdma(4 * gp)
            xdma(4 * gp + 1)
            if gp >= 2:
                sync.wait_ge(sem["s_mq"], 4 * (gp - 1))
                sync.wait_ge(sem["s_mk"], 4 * (gp - 1))
            sync.dma_start(out=t1_sb[:, gp % 2, :], in_=t1_d[:, gp * GT:(gp + 1) * GT]).then_inc(sem["d_t%d" % (gp % 2)], 16)
            sync.dma_start(out=t2_sb[:, gp % 2, :], in_=t2_d[:, gp * GT:(gp + 1) * GT]).then_inc(sem["d_t%d" % (gp % 2)], 16)
            xdma(4 * gp + 2)
            xdma(4 * gp + 3)

        def slice_hp(gp, hp):
            idx = 4 * gp + hp
            if idx >= 2:
                sync.wait_ge(sem["s_qmm"], idx - 1)
                sync.wait_ge(sem["s_kmm"], idx - 1)
            if gp == 0:
                sync.wait_ge(sem["d_wb0"], 64 * 4)
                sync.wait_ge(sem["d_wb1"], 64 * 4)
            sync.dma_start(out=wqr_sb[:, hp % 2, :, :],
                           in_=wqh_d.ap()[hp].rearrange("f p c -> p f c")).then_inc(sem["d_wq%d" % (hp % 2)], 16)
            if gp == 0:
                sync.wait_ge(sem["d_wb0"], 64 * 8)
                sync.wait_ge(sem["d_wb1"], 64 * 8)
            sync.dma_start(out=wkr_sb[:, hp % 2, :, :],
                           in_=wkh_d.ap()[hp].rearrange("f p c -> p f c")).then_inc(sem["d_wq%d" % (hp % 2)], 16)

        def slice_dmas(gp):
            for hp in range(4):
                slice_hp(gp, hp)

        def p3load(t3):
            sync.wait_ge(sem["s_cc"], t3 // 2 + 1)
            sync.wait_ge(sem["s_o3"], t3)
            sync.dma_start(out=t3_sb[:],
                           in_=cc_out.ap()[t3 * 128:(t3 + 1) * 128, :]).then_inc(sem["d_a3"], 16)
            sync.dma_start(out=x3_sb[:],
                           in_=x_own[t3 * 128:(t3 + 1) * 128, :]).then_inc(sem["d_a3"], 16)

        def p3out(t3):
            sync.wait_ge(sem["s_o3"], t3 + 1)
            sync.dma_start(out=out_d[t3 * 128:(t3 + 1) * 128, :],
                           in_=zg3_sb[:]).then_inc(sem["d_out"], 16)

        xdma(0)
        xdma(1)
        sync.dma_start(out=t1_sb[:, 0, :], in_=t1_d[:, 0:GT]).then_inc(sem["d_t0"], 16)
        sync.dma_start(out=t2_sb[:, 0, :], in_=t2_d[:, 0:GT]).then_inc(sem["d_t0"], 16)
        for csrc, cdst in [(bq_d, bq_sb[:]), (bk_d, bk_sb[:]), (bv_d, bv_sb[:]),
                           (idn_d, idn_sb[:]), (oblk_d, oblk_sb[:]),
                           (msk_d, msk_sb[:]),
                           (a1_d, a1_sb[:]), (a2_d, a2_sb[:]),
                           (g_d, g_sb[:]), (bb_d, bbc_sb[:]), (bo_d, bo_sb[:])]:
            sync.dma_start(out=cdst, in_=csrc).then_inc(sem["d_wc"], 16)
        def wload(i):
            w, f = CHUNKS[i]
            if i >= 4:
                sync.wait_ge(sem["s_cast"], i - 3)
            wsrc = {"wq": wq_d, "wk": wk_d}[w][f]
            stg4 = [pa_sb[:, 0, 0:512], pa_sb[:, 1, 0:512],
                    x13_sb[:, 0, 0:512], x13_sb[:, 1, 0:512]][i % 4]
            sync.dma_start(out=stg4,
                           in_=wsrc).then_inc(sem["d_ws%d" % (i % 4)], 16)

        for i in range(6):
            wload(i)
        xdma(2)
        for i in range(6, 10):
            wload(i)
        xdma(3)
        for i in range(10, 16):
            wload(i)
        slice_dmas(0)
        for g in range(NG):
            if g >= 2:
                p3load(2 * (g - 2))
            if g < 7:
                prep_lite(g + 1)
                slice_hp(g + 1, 0)
            if g >= 2:
                p3load(2 * (g - 2) + 1)
                p3out(2 * (g - 2))
                p3out(2 * (g - 2) + 1)
            if g < 7:
                slice_hp(g + 1, 1)
                slice_hp(g + 1, 2)
                slice_hp(g + 1, 3)
            if g == 7:
                p3load(12)
                p3load(13)
                p3out(12)
                p3out(13)
        for t3 in (14, 15):
            p3load(t3)
            p3out(t3)
        sync.wait_ge(sem["d_out"], 16 * 16)

    @block.vector
    def _(vector):
        vector = _Draining(vector)

        def cast_chunk(i):
            w, f = CHUNKS[i]
            if i < 16:
                vector.wait_ge(sem["d_ws%d" % (i % 4)], 16 * (i // 4 + 1))
            else:
                vector.wait_ge(sem["d_ws%d" % (4 + i % 4)], 16 * ((i - 16) // 4 + 1))
            stg = [pa_sb[:, 0, :], pa_sb[:, 1, :],
                   x13_sb[:, 0, :], x13_sb[:, 1, :]][i % 4]
            if w in ("wq", "wk"):
                if i >= 2:
                    vector.wait_ge(sem["d_wb%d" % (i % 2)], 64 * (i // 2))
                vector.tensor_copy(p_sb[i % 2][:, 0:512], stg[:, 0:512]).then_inc(sem["s_cast"], 1)
            elif w == "wv":
                vector.tensor_copy(wvr_sb[:, f, :], stg[:, 0:512]).then_inc(sem["s_cast"], 1)
            else:
                vector.tensor_copy(wo_bf[:, f, :], stg[:, 0:1024]).then_inc(sem["s_cast"], 1)

        def ln_a(gp, j):
            tt = 4 * gp + j
            vector.wait_ge(sem["d_x%d" % (tt % 2)], 16 * (tt // 2 + 1))
            if tt >= 2:
                vector.wait_ge(sem["s_trm"], 2 * tt - 2)
            xt = x_sb[:, tt % 2, :]
            vector.bn_stats(st_sb[:, 0:6], xt[:, 0:512])
            vector.bn_stats(st_sb[:, 6:12], xt[:, 512:1024])
            vector.bn_aggr(ag_sb[:, 0:2], st_sb[:, 0:6])
            vector.bn_aggr(ag_sb[:, 2:4], st_sb[:, 6:12])
            m1, v1, m2, v2 = (ag_sb[:, i:i + 1] for i in range(4))
            vector.tensor_add(sc_sb[:, 0:1], m1, m2)
            vector.tensor_scalar_mul(sc_sb[:, 12 + j:13 + j], sc_sb[:, 0:1], 0.5)
            vector.tensor_sub(sc_sb[:, 2:3], m1, m2)
            vector.tensor_mul(sc_sb[:, 3:4], sc_sb[:, 2:3], sc_sb[:, 2:3])
            vector.tensor_add(sc_sb[:, 4:5], v1, v2)
            vector.tensor_scalar_mul(sc_sb[:, 5:6], sc_sb[:, 4:5], 0.5)
            vector.scalar_tensor_tensor(sc_sb[:, 6:7], sc_sb[:, 3:4], 0.25,
                                        sc_sb[:, 5:6], ALU.mult, ALU.add)
            vector.tensor_scalar_add(wln_sb[:, j:j + 1], sc_sb[:, 6:7], 1e-5).then_inc(sem["s_w"], 1)

        def ln_b(gp, j):
            tt = 4 * gp + j
            xt = x_sb[:, tt % 2, :]
            vector.wait_ge(sem["s_r0"], tt + 1)
            r0 = r0_sb[:, j:j + 1]
            vector.tensor_mul(sc_sb[:, 7:8], r0, r0)
            vector.tensor_mul(sc_sb[:, 8:9], sc_sb[:, 7:8], wln_sb[:, j:j + 1])
            vector.tensor_scalar(sc_sb[:, 9:10], sc_sb[:, 8:9], -0.5, 1.5, ALU.mult, ALU.add)
            vector.tensor_mul(sc_sb[:, 10:11], r0, sc_sb[:, 9:10])
            vector.tensor_scalar(ubf_sb[:, tt % 2, :], xt, sc_sb[:, 12 + j:13 + j],
                                 sc_sb[:, 10:11], ALU.subtract, ALU.mult).then_inc(sem["s_ln"], 1)

        def onorm_block(g, hp):
            vector.wait_ge(sem["s_rr"], 4 * g + hp + 1)
            vector.tensor_mul(onog[0:32, hp, :], o_psA[0:32, :], rr_sb[:, 0, :])
            vector.tensor_mul(onog[32:64, hp, :], o_psA[32:64, :], rr_sb[:, 0, :])
            vector.tensor_mul(onog[64:96, hp, :], o_psB[0:32, :], rr_sb[:, 1, :])
            vector.tensor_mul(onog[96:128, hp, :], o_psB[32:64, :], rr_sb[:, 1, :]).then_inc(sem["s_onorm"], 1)

        def dve_stages(gp):
            st = []
            for j in range(4):
                st.append((("lna", j), lambda j=j: ln_a(gp, j)))
                st.append((("lnb", j), lambda j=j: ln_b(gp, j)))
            for j in range(4):
                tt = 4 * gp + j

                def eva(j=j, tt=tt):
                    if gp >= 1:
                        vector.wait_ge(sem["s_vmm"], 4 * (gp - 1) + j + 1)
                    vector.wait_ge(sem["s_trm"], 2 * tt + 1)
                    vector.tensor_copy(utg_sb[:, 0:4, j * 128:(j + 1) * 128],
                                       p1a_ps[:].rearrange("p (c k) -> p c k", c=4)).then_inc(sem["s_utg"], 1)

                def evb(j=j, tt=tt):
                    vector.wait_ge(sem["s_trm"], 2 * tt + 2)
                    vector.tensor_copy(utg_sb[:, 4:8, j * 128:(j + 1) * 128],
                                       p1b_ps[:].rearrange("p (c k) -> p c k", c=4)).then_inc(sem["s_utg"], 1)

                st.append((("eva", j), eva))
                st.append((("evb", j), evb))
            t1g, t2g = t1_sb[:, gp % 2, :], t2_sb[:, gp % 2, :]
            for hp in range(4):
                idx = 4 * gp + hp

                def mq(hp=hp, idx=idx):
                    vector.wait_ge(sem["s_qmm"], idx + 1)
                    vector.wait_ge(sem["s_cq"], idx)
                    vector.wait_ge(sem["d_t%d" % (gp % 2)], 32 * (gp // 2 + 1))
                    vector.tensor_scalar_add(tq_sb[:], p1a_ps[:], bq_sb[:, hp:hp + 1])
                    vector.scalar_tensor_tensor(m1q_sb[:], tq_sb[:], 0.125, t1g, ALU.mult, ALU.mult)
                    vector.scalar_tensor_tensor(m2q_sb[:], tq_sb[:], 0.125, t2g, ALU.mult, ALU.mult).then_inc(sem["s_mq"], 1)

                def mk(hp=hp, idx=idx):
                    vector.wait_ge(sem["s_kmm"], idx + 1)
                    vector.wait_ge(sem["s_ck"], idx)
                    vector.tensor_scalar_add(tk_sb[:], p1b_ps[:], bk_sb[:, hp:hp + 1])
                    vector.tensor_mul(m1k_sb[:], tk_sb[:], t1g)
                    vector.tensor_mul(m2k_sb[:], tk_sb[:], t2g).then_inc(sem["s_mk"], 1)

                def sqq(hp=hp, idx=idx):
                    vector.wait_ge(sem["s_cq"], idx + 1)
                    vector.tensor_copy(qhat[:, gp % 2, hp, :], p1a_ps[:]).then_inc(sem["s_qhe"], 1)
                    vector.tensor_mul(sqq_sb[:], qhat[:, gp % 2, hp, :],
                                      qhat[:, gp % 2, hp, :]).then_inc(sem["s_rq"], 1)

                def sqk(hp=hp, idx=idx):
                    vector.wait_ge(sem["s_ck"], idx + 1)
                    if idx >= 1:
                        vector.wait_ge(sem["s_nmm"], 2 * idx)
                    vector.tensor_copy(khat[:, hp, gp * GT:(gp + 1) * GT],
                                       p1b_ps[:]).then_inc(sem["s_khe"], 1)
                    vector.tensor_mul(sqk_sb[:], khat[:, hp, gp * GT:(gp + 1) * GT],
                                      khat[:, hp, gp * GT:(gp + 1) * GT]).then_inc(sem["s_rk"], 1)

                def rmq(hp=hp, idx=idx):
                    vector.wait_ge(sem["s_nmm"], 2 * idx + 1)
                    vector.reduce_max(nmaxq_sb[:, hp:hp + 1], p1a_ps[0:64, :], AXX).then_inc(sem["s_nmax"], 1)

                def rmk(hp=hp, idx=idx):
                    vector.wait_ge(sem["s_nmm"], 2 * idx + 2)
                    vector.reduce_max(nmaxk_sb[:, hp:hp + 1], p1b_ps[0:64, :], AXX).then_inc(sem["s_nmax"], 1)

                for key, fn in (("mq", mq), ("mk", mk), ("sqq", sqq),
                                ("sqk", sqk), ("rmq", rmq), ("rmk", rmk)):
                    st.append(((key, hp), fn))

            def b2():
                if gp == 0:
                    vector.tensor_copy(nk2_sb[:], nmaxk_sb[:])
                else:
                    vector.tensor_max(nk2_sb[:], nk2_sb[:], nmaxk_sb[:])
                vector.tensor_mul(bsc_sb[:, 0:4], nmaxq_sb[:], nk2_sb[:]).then_inc(sem["s_b2"], 1)

            def bfin():
                vector.wait_ge(sem["s_bexp"], gp + 1)
                vector.tensor_scalar_mul(bsc_sb[:, 8:12], bsc_sb[:, 4:8], -1.0)
                vector.tensor_copy(p1b_ps[32:64, 0:4], bsc_sb[32:64, 8:12])
                vector.tensor_tensor(bias_sb[0:32, gp % 2, :], bsc_sb[0:32, 8:12],
                                     p1b_ps[32:64, 0:4], ALU.min)
                vector.tensor_copy(bias_sb[32:64, gp % 2, :], bias_sb[0:32, gp % 2, :])
                vector.tensor_copy(bias_sb[64:128, gp % 2, :], bias_sb[0:64, gp % 2, :]).then_inc(sem["s_bias"], 1)

            st.append((("b2", 0), b2))
            st.append((("bfin", 0), bfin))
            for j in range(4):
                tt = 4 * gp + j

                def vev(j=j, tt=tt):
                    vector.wait_ge(sem["s_vmm"], tt + 1)
                    if gp == 0:
                        vp = [o_psA[:, 0:512], o_psB[:, 0:512],
                              s_pair[0][:, 0:512], s_pair[1][:, 0:512]][j]
                    else:
                        vp = (p1a_ps if j % 2 == 0 else p1b_ps)[:]
                    vector.tensor_add(
                        vau[:, :, tt, :].rearrange("p h (t c) -> p h t c", t=2)[:, :, :, 0:64],
                        vp.rearrange("p (h t c) -> p h t c", h=4, t=2),
                        bv_sb[:].rearrange("p (h t c) -> p h t c", h=4, t=2),
                    ).then_inc(sem["s_vev"], 1)

                st.append((("vev", j), vev))
            return st

        def pa_evac(g, j):
            tt = 4 * g + j
            vector.wait_ge(sem["s_womm"], tt + 1)
            if tt >= 2:
                vector.wait_ge(sem["d_pa%d" % (tt % 2)], 16 * (tt // 2))
            vector.tensor_copy(pa_sb[:, tt % 2, 0:512], p1a_ps[:])
            vector.tensor_copy(pa_sb[:, tt % 2, 512:1024], p1b_ps[:]).then_inc(sem["s_paev"], 1)

        def p3a(t3):
            vector.wait_ge(sem["d_a3"], 32 * (t3 + 1))
            vector.tensor_add(t3_sb[:], t3_sb[:], x3_sb[:]).then_inc(sem["s_t3"], 1)
            x1 = x13_sb[:, t3 % 2, :]
            vector.tensor_add(x1, t3_sb[:], bo_sb[:])
            vector.bn_stats(st3_sb[:, 0:6], x1[:, 0:512])
            vector.bn_stats(st3_sb[:, 6:12], x1[:, 512:1024])
            vector.bn_aggr(ag3_sb[:, 0:2], st3_sb[:, 0:6])
            vector.bn_aggr(ag3_sb[:, 2:4], st3_sb[:, 6:12])
            m1, v1, m2, v2 = (ag3_sb[:, i:i + 1] for i in range(4))
            vector.tensor_add(sc3_sb[:, 0:1], m1, m2)
            vector.tensor_scalar_mul(m3_sb[:, t3 % 2:t3 % 2 + 1], sc3_sb[:, 0:1], 0.5)
            vector.tensor_sub(sc3_sb[:, 2:3], m1, m2)
            vector.tensor_mul(sc3_sb[:, 3:4], sc3_sb[:, 2:3], sc3_sb[:, 2:3])
            vector.tensor_add(sc3_sb[:, 4:5], v1, v2)
            vector.tensor_scalar_mul(sc3_sb[:, 5:6], sc3_sb[:, 4:5], 0.5)
            vector.scalar_tensor_tensor(sc3_sb[:, 6:7], sc3_sb[:, 3:4], 0.25,
                                        sc3_sb[:, 5:6], ALU.mult, ALU.add)
            vector.tensor_scalar(w3_sb[:, t3 % 2:t3 % 2 + 1], sc3_sb[:, 6:7],
                                 1.00001, 1e-10, ALU.mult, ALU.add).then_inc(sem["s_w3"], 1)

        def p3b(t3):
            x1 = x13_sb[:, t3 % 2, :]
            vector.wait_ge(sem["s_r3"], t3 + 1)
            r0 = r3_sb[:, t3 % 2:t3 % 2 + 1]
            vector.tensor_mul(sc3_sb[:, 7:8], r0, r0)
            vector.tensor_mul(sc3_sb[:, 8:9], sc3_sb[:, 7:8], w3_sb[:, t3 % 2:t3 % 2 + 1])
            vector.tensor_scalar(sc3_sb[:, 9:10], sc3_sb[:, 8:9], -0.5, 1.5, ALU.mult, ALU.add)
            vector.tensor_mul(sc3_sb[:, 10:11], r0, sc3_sb[:, 9:10])
            if t3 >= 1:
                vector.wait_ge(sem["d_out"], 16 * t3)
            vector.tensor_scalar(zg3_sb[:], x1, m3_sb[:, t3 % 2:t3 % 2 + 1],
                                 sc3_sb[:, 10:11], ALU.subtract, ALU.mult)
            vector.tensor_mul(zg3_sb[:], zg3_sb[:], g_sb[:])
            vector.tensor_add(t3_sb[:], x1, bbc_sb[:])
            vector.tensor_add(zg3_sb[:], zg3_sb[:], t3_sb[:]).then_inc(sem["s_o3"], 1)

        st0 = dve_stages(0)
        st0k = sorted(enumerate(st0), key=lambda t: ((min(_sched(0)[t[1][0]], 16), t[0])
                                                     if t[1][0][0] not in ("v", "vev")
                                                     else (15, 1000 + t[1][0][1])))
        early = [fn for _i, (key, fn) in st0k if key[0] in ("lna", "lnb", "eva", "evb")]
        late = [fn for _i, (key, fn) in st0k if key[0] not in ("lna", "lnb", "eva", "evb")]
        cast_chunk(0)
        cast_chunk(1)
        vector.wait_ge(sem["d_wc"], 16 * NCONST)
        vector.memset(vau[:, :, :, 64:96], 1.0)
        vector.memset(vau[:, :, :, 160:192], 1.0).then_inc(sem["s_onorm"], 1)
        for i in range(2, 28):
            cast_chunk(i)
            if i - 2 < len(early):
                early[i - 2]()
        for _fn in late:
            _fn()
        for g in range(NG):
            n = 16 * (g + 1)
            L = 4 * g + 4
            items = []
            if g < 7:
                pos = _sched(g)
                for seq, (key, fn) in enumerate(dve_stages(g + 1)):
                    items.append((pos[key], 1, seq, fn))
            for hp in range(4):
                items.append((min((hp + 1) * L, n), 2, hp,
                              lambda g=g, hp=hp: onorm_block(g, hp)))
            if g >= 1:
                for j in range(4):
                    items.append((2 * j + 2, 3, j, lambda g=g, j=j: pa_evac(g - 1, j)))
            if g >= 2:
                t3 = 2 * (g - 2)
                items.append((3, 1, 1000, lambda t3=t3: p3a(t3)))
                items.append((8, 1, 1001, lambda t3=t3: p3b(t3)))
                items.append((13, 1, 1002, lambda t3=t3: p3a(t3 + 1)))
                items.append((18, 1, 1003, lambda t3=t3: p3b(t3 + 1)))
            if g == 7:
                items.append((40, 1, 1010, lambda: p3a(12)))
                items.append((45, 1, 1011, lambda: p3b(12)))
                items.append((50, 1, 1012, lambda: p3a(13)))
                items.append((55, 1, 1013, lambda: p3b(13)))
            items.sort(key=lambda x: (x[0], x[1], x[2]))
            for _p, _q, _s, fn in items:
                fn()
        for j in range(4):
            pa_evac(7, j)
        for t3 in (14, 15):
            p3a(t3)
            p3b(t3)

    @block.scalar
    def _(scalar):
        scalar = _Draining(scalar)
        scalar.wait_ge(sem["d_wc"], 16 * NCONST)

        def r0_tile(gp, j):
            tt = 4 * gp + j
            scalar.wait_ge(sem["s_w"], tt + 1)
            scalar.activation(r0_sb[:, j:j + 1], wln_sb[:, j:j + 1], AF.Ln)
            scalar.activation(r0_sb[:, j:j + 1], r0_sb[:, j:j + 1], AF.Exp, scale=-0.5).then_inc(sem["s_r0"], 1)

        def act_stages(gp):
            st = []
            for j in range(4):
                st.append((("r0", j), lambda j=j: r0_tile(gp, j)))

            def bexp():
                scalar.wait_ge(sem["s_b2"], gp + 1)
                scalar.activation(bsc_sb[:, 4:8], bsc_sb[:, 0:4], AF.Ln)
                scalar.activation(bsc_sb[:, 4:8], bsc_sb[:, 4:8], AF.Exp, scale=0.5).then_inc(sem["s_bexp"], 1)

            st.append((("bexp", 0), bexp))
            return st

        def p3_act(t3):
            scalar.wait_ge(sem["s_w3"], t3 + 1)
            scalar.activation(r3_sb[:, t3 % 2:t3 % 2 + 1], w3_sb[:, t3 % 2:t3 % 2 + 1], AF.Ln)
            scalar.activation(r3_sb[:, t3 % 2:t3 % 2 + 1], r3_sb[:, t3 % 2:t3 % 2 + 1],
                              AF.Exp, scale=-0.5).then_inc(sem["s_r3"], 1)

        for _fn in _chain_order(act_stages(0)):
            _fn()
        for g in range(NG):
            n = 16 * (g + 1)
            items = []
            if g < 7:
                pos = _sched(g)
                for seq, (key, fn) in enumerate(act_stages(g + 1)):
                    items.append((pos[key], seq, fn))
            if g >= 2:
                t3 = 2 * (g - 2)
                items.append((6, 1000, lambda t3=t3: p3_act(t3)))
                items.append((16, 1001, lambda t3=t3: p3_act(t3 + 1)))
            if g == 7:
                items.append((43, 1010, lambda: p3_act(12)))
                items.append((53, 1011, lambda: p3_act(13)))
            items.sort(key=lambda x: (x[0], x[1]))
            si = 0
            t = 0
            for i in range(UOFF[g], UOFF[g + 1]):
                gg, hp, kt = UNITS[i]
                scalar.wait_ge(sem["s_qk2"], i + 1)
                if i >= 3:
                    scalar.wait_ge(sem["s_pv"], i - 2)
                if kt == 0 and hp == 0:
                    scalar.wait_ge(sem["s_bias"], g + 1)
                dq0 = 128 * (kt - 4 * g) if kt >= 4 * g else 0
                if dq0 > 0:
                    esrc = s_pair[i % 2][:].rearrange("p (h q) -> p h q", h=2)[:, :, dq0:512]
                    edst = p_sb[i % 3][:].rearrange("p (h q) -> p h q", h=2)[:, :, dq0:512]
                else:
                    esrc, edst = s_pair[i % 2][:], p_sb[i % 3][:]
                scalar.activation(edst, esrc, AF.Exp,
                                  bias=bias_sb[:, g % 2, hp:hp + 1]).then_inc(sem["s_exp"], 1)
                if kt == 4 * g + 3:
                    scalar.wait_ge(sem["s_pv"], ULAST[(g, hp)] + 1)
                    scalar.activation(rr_sb[:, 0, :], o_psA[64:96, :], AF.Ln)
                    scalar.activation(rr_sb[:, 0, :], rr_sb[:, 0, :], AF.Exp, scale=-1.0)
                    scalar.activation(rr_sb[:, 1, :], o_psB[64:96, :], AF.Ln)
                    scalar.activation(rr_sb[:, 1, :], rr_sb[:, 1, :], AF.Exp, scale=-1.0).then_inc(sem["s_rr"], 1)
                while si < len(items) and items[si][0] <= t:
                    items[si][2]()
                    si += 1
                t += 1
            while si < len(items):
                items[si][2]()
                si += 1
        for t3 in (14, 15):
            p3_act(t3)

    @block.gpsimd
    def _(gpsimd):
        gpsimd = _Draining(gpsimd)
        def wchunk_wb(i):
            w, f = CHUNKS[i]
            gpsimd.wait_ge(sem["s_cast"], i + 1)
            dstd = wqh_d if w == "wq" else wkh_d
            for hp in range(4):
                gpsimd.dma_start(out=dstd.ap()[hp, f, :, :],
                                 in_=p_sb[i % 2][:, hp * 128:(hp + 1) * 128]).then_inc(sem["d_wb%d" % (i % 2)], 16)

        def wload2(i):
            w, f = CHUNKS[i]
            gpsimd.wait_ge(sem["s_cast"], i - 3)
            wsrc = {"wv": wv_d, "wo": wo_d}[w][f]
            width = 1024 if w == "wo" else 512
            stg4 = [pa_sb[:, 0, :], pa_sb[:, 1, :],
                    x13_sb[:, 0, :], x13_sb[:, 1, :]][i % 4]
            gpsimd.dma_start(out=stg4[:, 0:width],
                             in_=wsrc).then_inc(sem["d_ws%d" % (4 + i % 4)], 16)

        for j in range(14):
            wchunk_wb(j)
        wload2(16)
        wchunk_wb(14)
        wload2(17)
        wchunk_wb(15)
        for i in range(18, 28):
            wload2(i)
        gpsimd.wait_ge(sem["d_wc"], 16 * NCONST)

        for g in range(NG):
            for j in range(4):
                tt = 4 * g + j
                gpsimd.wait_ge(sem["s_paev"], tt + 1)
                gpsimd.dma_start(out=cc_in.ap()[tt * 128:(tt + 1) * 128, :],
                                 in_=pa_sb[:, tt % 2, :]).then_inc(sem["d_pa%d" % (tt % 2)], 16)
            gpsimd.wait_ge(sem["d_pa0"], 16 * 2 * (g + 1))
            gpsimd.wait_ge(sem["d_pa1"], 16 * 2 * (g + 1))
            gpsimd.collective_compute(
                "ReduceScatter", ALU.add,
                replica_groups=[[0, 1], [2, 3], [4, 5], [6, 7]],
                ins=[cc_in.ap()[g * GT:(g + 1) * GT, :].opt()],
                outs=[cc_out.ap()[g * 256:(g + 1) * 256, :].opt()],
            ).then_inc(sem["s_cc"], 1)

    @block.tensor
    def _(tensor):
        tensor.wait_ge(sem["d_wc"], 16 * NCONST)

        def pe_stages(gp):
            st = []
            for j in range(4):
                tt = 4 * gp + j

                def tma(j=j, tt=tt):
                    tensor.wait_ge(sem["s_ln"], tt + 1)
                    if j == 0:
                        if gp >= 2:
                            tensor.wait_ge(sem["s_paev"], 4 * (gp - 2) + 4)
                        if gp >= 1:
                            tensor.wait_ge(sem["s_vev"], 4 * (gp - 1) + 3)
                    else:
                        tensor.wait_ge(sem["s_utg"], 2 * (tt - 1) + 1)
                    for c in range(4):
                        ins = tensor.matmul(p1a_ps[:, c * 128:(c + 1) * 128],
                                            ubf_sb[:, tt % 2, c * 128:(c + 1) * 128],
                                            idn_sb[:], start=True, stop=True)
                    ins.then_inc(sem["s_trm"], 1)

                def tmb(j=j, tt=tt):
                    if j == 0:
                        if gp >= 2:
                            tensor.wait_ge(sem["s_paev"], 4 * (gp - 2) + 4)
                        if gp >= 1:
                            tensor.wait_ge(sem["s_vev"], 4 * (gp - 1) + 4)
                    else:
                        tensor.wait_ge(sem["s_utg"], 2 * (tt - 1) + 2)
                    for c in range(4):
                        ins = tensor.matmul(p1b_ps[:, c * 128:(c + 1) * 128],
                                            ubf_sb[:, tt % 2, (c + 4) * 128:(c + 5) * 128],
                                            idn_sb[:], start=True, stop=True)
                    ins.then_inc(sem["s_trm"], 1)

                st.append((("tma", j), tma))
                st.append((("tmb", j), tmb))
            for hp in range(4):
                idx = 4 * gp + hp
                sidx = 2 * gp + hp // 2

                def qmm(hp=hp, idx=idx, sidx=sidx):
                    if hp == 0:
                        tensor.wait_ge(sem["s_utg"], 8 * (gp + 1))
                    tensor.wait_ge(sem["d_wq%d" % (hp % 2)], 32 * (sidx + 1))
                    if idx >= 1:
                        tensor.wait_ge(sem["s_nmax"], 2 * (idx - 1) + 1)
                    for f in range(8):
                        ins = tensor.matmul(p1a_ps[:], wqr_sb[:, hp % 2, f, :],
                                            utg_sb[:, f, :], start=(f == 0), stop=(f == 7))
                    ins.then_inc(sem["s_qmm"], 1)

                def kmm(hp=hp, idx=idx):
                    if idx >= 1:
                        tensor.wait_ge(sem["s_nmax"], 2 * (idx - 1) + 2)
                    for f in range(8):
                        ins = tensor.matmul(p1b_ps[:], wkr_sb[:, hp % 2, f, :],
                                            utg_sb[:, f, :], start=(f == 0), stop=(f == 7))
                    ins.then_inc(sem["s_kmm"], 1)

                def cmbq(hp=hp, idx=idx):
                    tensor.wait_ge(sem["s_mq"], idx + 1)
                    tensor.matmul(p1a_ps[:], a1_sb[:], m1q_sb[:], start=True, stop=False)
                    tensor.matmul(p1a_ps[:], a2_sb[:], m2q_sb[:], start=False, stop=True).then_inc(sem["s_cq"], 1)

                def cmbk(hp=hp, idx=idx):
                    tensor.wait_ge(sem["s_mk"], idx + 1)
                    tensor.matmul(p1b_ps[:], a1_sb[:], m1k_sb[:], start=True, stop=False)
                    tensor.matmul(p1b_ps[:], a2_sb[:], m2k_sb[:], start=False, stop=True).then_inc(sem["s_ck"], 1)

                def nq(hp=hp, idx=idx):
                    tensor.wait_ge(sem["s_qhe"], idx + 1)
                    tensor.wait_ge(sem["s_rq"], idx + 1)
                    tensor.matmul(p1a_ps[0:64, :], oblk_sb[:, 0:64], sqq_sb[:],
                                  start=True, stop=True).then_inc(sem["s_nmm"], 1)

                def nk(hp=hp, idx=idx):
                    tensor.wait_ge(sem["s_khe"], idx + 1)
                    tensor.wait_ge(sem["s_rk"], idx + 1)
                    tensor.matmul(p1b_ps[0:64, :], oblk_sb[:, 0:64], sqk_sb[:],
                                  start=True, stop=True).then_inc(sem["s_nmm"], 1)

                for key, fn in (("qmm", qmm), ("kmm", kmm), ("cmbq", cmbq),
                                ("cmbk", cmbk), ("nq", nq), ("nk", nk)):
                    st.append(((key, hp), fn))
            for j in range(4):
                tt = 4 * gp + j

                def vmm(j=j, tt=tt):
                    if gp == 0:
                        vp = [o_psA[:, 0:512], o_psB[:, 0:512],
                              s_pair[0][:, 0:512], s_pair[1][:, 0:512]][j]
                        if j == 0:
                            tensor.wait_ge(sem["s_cast"], 24)
                            tensor.wait_ge(sem["s_utg"], 8)
                    else:
                        vp = (p1a_ps if j % 2 == 0 else p1b_ps)[:]
                        if j < 2:
                            tensor.wait_ge(sem["s_nmax"], 8 * gp + 7 + j)
                            if j == 1:
                                tensor.wait_ge(sem["s_bias"], gp + 1)
                        else:
                            tensor.wait_ge(sem["s_vev"], tt - 1)
                    for f in range(8):
                        ins = tensor.matmul(vp, utg_sb[:, f, j * 128:(j + 1) * 128],
                                            wvr_sb[:, f, :], start=(f == 0), stop=(f == 7))
                    ins.then_inc(sem["s_vmm"], 1)

                st.append((("v", j), vmm))
            return st

        def emit_pv(u):
            g, hp, kt = UNITS[u]
            first, last = (kt == 0), (kt == 4 * g + 3)
            diag = kt >= 4 * g
            q0 = 128 * (kt - 4 * g) if diag else 0
            pa, pb = p_sb[u % 3][:, q0:512], p_sb[u % 3][:, 512 + q0:1024]
            tensor.wait_ge(sem["s_exp"], u + 1)
            if diag:
                tensor.wait_ge(sem["s_vev"], kt + 1)
            if first:
                tensor.wait_ge(sem["s_onorm"], 4 * g + hp + 1)
            tensor.matmul(o_psA[0:96, q0:512], vau[:, hp, kt, 0:96], pa,
                          start=first, stop=last, skip_group_check=True)
            tensor.matmul(o_psB[0:96, q0:512], vau[:, hp, kt, 96:192], pb,
                          start=first, stop=last, skip_group_check=True).then_inc(sem["s_pv"], 1)

        def wo_stage(gm, j):
            tt = 4 * gm + j
            if gm == 0 and j == 0:
                tensor.wait_ge(sem["s_cast"], NWC)
            if j == 0:
                tensor.wait_ge(sem["s_onorm"], 4 * (gm + 1) + 1)
                tensor.wait_ge(sem["s_vev"], min(4 * (gm + 2), 32))
            else:
                tensor.wait_ge(sem["s_paev"], tt)
            for half in range(2):
                wo_ps = p1a_ps if half == 0 else p1b_ps
                for hp in range(4):
                    ins = tensor.matmul(wo_ps[:],
                                        onog[:, hp, j * 128:(j + 1) * 128],
                                        wo_bf[:, hp, half * 512:(half + 1) * 512],
                                        start=(hp == 0), stop=(hp == 3), skip_group_check=True)
            ins.then_inc(sem["s_womm"], 1)

        for _fn in _chain_order(pe_stages(0)):
            _fn()
        for g in range(NG):
            items = []
            if g < 7:
                pos = _sched(g)
                for seq, (key, fn) in enumerate(pe_stages(g + 1)):
                    items.append((pos[key], seq, fn))
            if g >= 1:
                for j in range(4):
                    items.append((2 * j + 1, -100 + j, lambda g=g, j=j: wo_stage(g - 1, j)))
            items.sort(key=lambda x: (x[0], x[1]))
            si = 0
            t = 0
            gc2 = (g % 2)
            pend = None
            u = UOFF[g]
            for hp in range(4):
                for kt in range(4 * g + 4):
                    if u >= 2:
                        tensor.wait_ge(sem["s_exp"], u - 1)
                    if kt == 0:
                        tensor.wait_ge(sem["s_qhe"], 4 * g + hp + 1)
                        tensor.wait_ge(sem["s_khe"], 4 * g + hp + 1)
                    if g == 0 and u < 2:
                        tensor.wait_ge(sem["s_vev"], 3 + u)
                    kc = slice(kt * 128, (kt + 1) * 128)
                    diag = kt >= 4 * g
                    q0 = 128 * (kt - 4 * g) if diag else 0
                    tensor.matmul(s_pair[u % 2][:, q0:512], khat[0:64, hp, kc],
                                  qhat[0:64, gc2, hp, q0:512], start=True, stop=not diag,
                                  skip_group_check=True)
                    ins2 = tensor.matmul(s_pair[u % 2][:, 512 + q0:1024], khat[64:128, hp, kc],
                                         qhat[64:128, gc2, hp, q0:512], start=True, stop=not diag,
                                         skip_group_check=True)
                    if diag:
                        tensor.matmul(s_pair[u % 2][:, q0:q0 + 128], idn_sb[:],
                                      msk_sb[:], start=False, stop=True, skip_group_check=True)
                        tensor.matmul(s_pair[u % 2][:, 512 + q0:512 + q0 + 128], idn_sb[:],
                                      msk_sb[:], start=False, stop=True,
                                      skip_group_check=True).then_inc(sem["s_qk2"], 1)
                    else:
                        ins2.then_inc(sem["s_qk2"], 1)
                    if pend is not None:
                        emit_pv(pend)
                    pend = u
                    u += 1
                    while si < len(items) and items[si][0] <= t:
                        items[si][2]()
                        si += 1
                    t += 1
            emit_pv(pend)
            while si < len(items):
                items[si][2]()
                si += 1
        for j in range(4):
            wo_stage(7, j)

    return nc, ctx


_PROG = None


def _rope_tables():
    inv_freq = (1.0 / 10000.0 ** (np.arange(0, DK, 2, dtype=np.float32) / np.float32(DK))).astype(np.float32)
    pos = np.arange(S, dtype=np.float32)
    fr = (pos[:, None] * inv_freq[None, :]).astype(np.float32)
    c = np.cos(fr.astype(np.float64)).astype(np.float32).T
    s = np.sin(fr.astype(np.float64)).astype(np.float32).T
    t1 = np.ascontiguousarray(np.concatenate([c, c, -s, -s], axis=0))
    t2 = np.ascontiguousarray(np.concatenate([s, s, c, c], axis=0))
    return t1, t2


def kernel(**inputs):
    global _PROG
    if _PROG is None:
        _PROG = _build()
    nc, _ctx = _PROG

    x = np.asarray(inputs["L0_attn_x"], np.float32)
    Wq = np.asarray(inputs["L0_attn_Wq"], np.float32)
    Wk = np.asarray(inputs["L0_attn_Wk"], np.float32)
    Wv = np.asarray(inputs["L0_attn_Wv"], np.float32)
    Wo = np.asarray(inputs["L0_attn_Wo"], np.float32)
    bq = np.asarray(inputs["L0_attn_bq"], np.float32)
    bk = np.asarray(inputs["L0_attn_bk"], np.float32)
    bv = np.asarray(inputs["L0_attn_bv"], np.float32)
    bo = np.asarray(inputs["L0_attn_bo"], np.float32)
    ln_g = np.asarray(inputs["L0_ffn_ln_g"], np.float32)
    ln_b = np.asarray(inputs["L0_ffn_ln_b"], np.float32)

    t1, t2 = _rope_tables()
    kk = np.arange(128)[:, None]
    qq = np.arange(GT)[None, :]
    kk128 = np.arange(128)[:, None]
    qq128 = np.arange(128)[None, :]
    msk = np.where(kk128 <= qq128, np.float32(0), np.float32(-1e9)).astype(BF)
    idn = np.eye(128, dtype=np.float32).astype(BF)
    oblk = np.zeros((128, 64), np.float32)
    oblk[0:64, 0:32] = 1.0
    oblk[64:128, 32:64] = 1.0
    oblk = oblk.astype(BF)
    map1 = np.concatenate([np.arange(32), np.arange(32, 64) + 32,
                           np.arange(64, 96) - 64, np.arange(96, 128) - 32])
    map2 = np.concatenate([np.arange(32) + 32, np.arange(32, 64) + 64,
                           np.arange(64, 96) - 32, np.arange(96, 128)])
    a1 = np.zeros((128, 128), np.float32)
    a1[np.arange(128), map1] = 1.0
    a2 = np.zeros((128, 128), np.float32)
    a2[np.arange(128), map2] = 1.0
    a1 = a1.astype(BF)
    a2 = a2.astype(BF)
    g_bc = np.ascontiguousarray(np.broadcast_to(ln_g, (128, D)), dtype=np.float32)
    b_bc = np.ascontiguousarray(np.broadcast_to(ln_b, (128, D)), dtype=np.float32)
    bo_bc = np.ascontiguousarray(np.broadcast_to(bo, (128, D)), dtype=np.float32)

    in_maps = []
    for c in range(8):
        b, hg = c // 2, c % 2
        heads = [hg * 8 + i for i in range(8)]
        perm = []
        for hp in range(4):
            a, b2 = heads[2 * hp], heads[2 * hp + 1]
            perm += (list(a * 64 + np.arange(0, 64, 2)) + list(b2 * 64 + np.arange(0, 64, 2))
                     + list(a * 64 + np.arange(1, 64, 2)) + list(b2 * 64 + np.arange(1, 64, 2)))
        perm = np.asarray(perm)
        vcols = np.arange(hg * 512, hg * 512 + 512)
        worows = []
        for hp in range(4):
            a, b2 = heads[2 * hp], heads[2 * hp + 1]
            worows += list(a * 64 + np.arange(64)) + list(b2 * 64 + np.arange(64))
        worows = np.asarray(worows)
        own_rows = np.concatenate([512 * gg + 256 * hg + np.arange(256) for gg in range(8)])
        in_maps.append({
            "x_b": np.ascontiguousarray(x[b]),
            "x_own": np.ascontiguousarray(x[b][own_rows]),
            "wq_d": np.ascontiguousarray(Wq[:, perm]).reshape(8, 128, 512),
            "wk_d": np.ascontiguousarray(Wk[:, perm]).reshape(8, 128, 512),
            "wv_d": np.ascontiguousarray(Wv[:, vcols]).reshape(8, 128, 512),
            "wo_d": np.ascontiguousarray(Wo[worows, :]).reshape(4, 128, 1024),
            "bq_d": np.ascontiguousarray(bq[perm].reshape(4, 128).T),
            "bk_d": np.ascontiguousarray(bk[perm].reshape(4, 128).T),
            "bv_d": np.ascontiguousarray(np.broadcast_to(bv[vcols], (128, 512)), dtype=np.float32),
            "t1_d": t1, "t2_d": t2, "msk_d": np.ascontiguousarray(msk),
            "idn_d": idn, "oblk_d": oblk,
            "a1_d": a1, "a2_d": a2,
            "g_d": g_bc, "bb_d": b_bc, "bo_d": bo_bc,
        })

    res = run_bass_kernel_spmd(nc, in_maps, core_ids=list(range(8)))
    out = np.empty((B, S, D), np.float32)
    for c in range(8):
        b, hg = c // 2, c % 2
        o = np.asarray(res.results[c]["out"]).reshape(8, 256, D)
        for gg in range(8):
            out[b, 512 * gg + 256 * hg: 512 * gg + 256 * (hg + 1)] = o[gg]
    return out


# version 26
# speedup vs baseline: 1.1871x; 1.1871x over previous
import numpy as np
import ml_dtypes
from contextlib import ExitStack

import concourse.bass as bass
import concourse.mybir as mybir
from concourse.bass_utils import run_bass_kernel_spmd

F32 = mybir.dt.float32
BF16 = mybir.dt.bfloat16
AF = mybir.ActivationFunctionType
ALU = mybir.AluOpType
AXX = mybir.AxisListType.X
BF = ml_dtypes.bfloat16

B, S, D, H, DK = 4, 4096, 1024, 16, 64
NG, GT = 8, 512
NT = S // 128
NHP = 4

UNITS = [(g, hp, kt) for g in range(NG) for hp in range(NHP) for kt in range(4 * g + 4)]
NU = len(UNITS)
ULAST = {}
for i, (g, hp, kt) in enumerate(UNITS):
    ULAST[(g, hp)] = i
UOFF = [0] * (NG + 1)
for g in range(NG):
    UOFF[g + 1] = UOFF[g] + 4 * (4 * g + 4)

def _sched(g):
    n = 16 * (g + 1)
    s = max(17, n - 50)
    P = {}
    lpos = {"lna": (0, 1, 10, 11), "r0": (1, 2, 11, 12), "lnb": (2, 3, 12, 13),
            "tma": (9, 10, 13, 14), "tmb": (9, 10, 13, 14),
            "eva": (10, 11, 14, 15), "evb": (10, 11, 14, 15)}
    for k, ps in lpos.items():
        for j in range(4):
            P[(k, j)] = ps[j]
    for hp in range(4):
        b = s + 10 * hp
        for k, off in (("qmm", 0), ("kmm", 1), ("tq", 1), ("mq", 2), ("tk", 3),
                       ("mk", 4), ("cmbq", 4), ("cmbk", 5), ("qhe", 5), ("khe", 6),
                       ("sqq", 6), ("sqk", 7), ("nq", 8), ("nk", 9), ("rmq", 9), ("rmk", 10)):
            P[(k, hp)] = b + off
    P[("b2", 0)] = s + 41
    P[("bexp", 0)] = s + 42
    P[("bfin", 0)] = s + 43
    for j, p in ((0, s + 40), (1, s + 44), (2, s + 45), (3, s + 46)):
        P[("v", j)] = p
    for j, p in ((0, s + 41), (1, s + 45), (2, s + 46), (3, s + 47)):
        P[("vev", j)] = p
    return {k: min(p, n) for k, p in P.items()}


ACT_TR = False
def _chain_order(stages):
    pos = _sched(0)
    n = 16

    def key_of(t):
        i, (key, _fn) = t
        if key[0] == "v":
            return (15, 100 + key[1])
        if key[0] == "vev":
            return (15, 1000 + key[1])
        return (min(pos[key], n), i)

    return [fn for _i, (key, fn) in sorted(enumerate(stages), key=key_of)]


NWC = 28
NCONST = 11

_DRAIN_OPS = {
    "tensor_copy", "tensor_tensor", "tensor_add", "tensor_sub", "tensor_mul",
    "tensor_max", "tensor_scalar", "tensor_single_scalar", "tensor_scalar_add",
    "tensor_scalar_sub", "tensor_scalar_mul", "tensor_scalar_max", "tensor_scalar_min",
    "tensor_reduce", "reduce_max", "reduce_sum", "scalar_tensor_tensor",
    "bn_stats", "bn_aggr", "reciprocal", "memset", "memzero", "activation",
    "add", "mul", "copy", "square", "sqrt", "select", "transpose",
}
_DRAINS_ON = [True]


def _ap_tensor_names(args):
    names = []
    for a in args:
        t = getattr(a, "tensor", None)
        if t is not None:
            names.append(getattr(t, "name", str(t)))
    return names


class _Draining:

    def __init__(self, eng):
        self._e = eng
        self._written = set()
        self._read = set()

    def __getattr__(self, name):
        attr = getattr(self._e, name)
        if name in _DRAIN_OPS and _DRAINS_ON[0]:
            def wrapped(*a, **k):
                names = _ap_tensor_names(list(a) + list(k.values()))
                out = names[0] if names else None
                need = any(n in self._written for n in names) or (out in self._read)
                if need:
                    self._e.drain()
                    self._written.clear()
                    self._read.clear()
                if out is not None:
                    self._written.add(out)
                for n in names[1:]:
                    self._read.add(n)
                return attr(*a, **k)
            return wrapped
        return attr


def _build(drains=True):
    _DRAINS_ON[0] = drains
    nc = bass.Bass("TRN2", target_bir_lowering=False, debug=False)

    def din(name, shape, dt):
        return nc.dram_tensor(name, shape, dt, kind="ExternalInput").ap()

    x_b = din("x_b", [S, D], F32)
    x_own = din("x_own", [S // 2, D], F32)
    wq_d = din("wq_d", [8, 128, 512], F32)
    wk_d = din("wk_d", [8, 128, 512], F32)
    wv_d = din("wv_d", [8, 128, 512], F32)
    wo_d = din("wo_d", [4, 128, 1024], F32)
    bq_d = din("bq_d", [128, 4], F32)
    bk_d = din("bk_d", [128, 4], F32)
    bv_d = din("bv_d", [128, 512], F32)
    t1_d = din("t1_d", [128, S], F32)
    t2_d = din("t2_d", [128, S], F32)
    msk_d = din("msk_d", [128, 128], BF16)
    idn_d = din("idn_d", [128, 128], BF16)
    oblk_d = din("oblk_d", [128, 64], BF16)
    g_d = din("g_d", [128, D], F32)
    bb_d = din("bb_d", [128, D], F32)
    bo_d = din("bo_d", [128, D], F32)
    a1_d = din("a1_d", [128, 128], BF16)
    a2_d = din("a2_d", [128, 128], BF16)
    out_d = nc.dram_tensor("out", [S // 2, D], F32, kind="ExternalOutput").ap()
    cc_in = nc.dram_tensor("cc_in", [S, D], F32)
    cc_out = nc.dram_tensor("cc_out", [S // 2, D], F32)
    wqh_d = nc.dram_tensor("wqh_d", [4, 8, 128, 128], BF16)
    wkh_d = nc.dram_tensor("wkh_d", [4, 8, 128, 128], BF16)

    ctx = ExitStack()
    sem = {}
    for name in ["d_ws0", "d_ws1", "d_ws2", "d_ws3", "d_ws4", "d_ws5", "d_ws6", "d_ws7",
                 "d_wc", "d_wb0", "d_wb1", "s_trm", "s_utg", "d_x0", "d_x1", "d_t0", "d_t1",
                 "d_tr0", "d_tr1", "d_wq0", "d_wq1",
                 "d_pa0", "d_pa1", "d_a3", "d_out",
                 "s_cast", "s_w", "s_r0", "s_ln",
                 "s_qmm", "s_kmm", "s_qe", "s_ke",
                 "s_mq", "s_mk", "s_cq", "s_ck", "s_qhe", "s_khe",
                 "s_rq", "s_rk", "s_nmm", "s_nmax", "s_vmm", "s_vev",
                 "s_b2", "s_bexp", "s_bias",
                 "s_exp", "s_qk2", "s_pv", "s_rr", "s_onorm",
                 "s_womm", "s_paev", "s_cc",
                 "s_t3", "s_x13", "s_w3", "s_r3", "s_z3", "s_zg3", "s_o3"]:
        sem[name] = ctx.enter_context(nc.semaphore(name))

    gsb = lambda n, sh, dt: ctx.enter_context(nc.sbuf_tensor(n, sh, dt))
    idn_sb = gsb("idn_sb", [128, 128], BF16)
    oblk_sb = gsb("oblk_sb", [128, 64], BF16)
    a1_sb = gsb("a1_sb", [128, 128], BF16)
    a2_sb = gsb("a2_sb", [128, 128], BF16)
    bias_sb = gsb("bias_sb", [128, 2, 4], F32)
    nmaxq_sb = gsb("nmaxq_sb", [64, 4], F32)
    nmaxk_sb = gsb("nmaxk_sb", [64, 4], F32)
    nk2_sb = gsb("nk2_sb", [64, 4], F32)
    bsc_sb = gsb("bsc_sb", [64, 12], F32)
    bq_sb = gsb("bq_sb", [128, 4], F32)
    bk_sb = gsb("bk_sb", [128, 4], F32)
    bv_sb = gsb("bv_sb", [128, 512], F32)
    khat = gsb("khat", [128, 4, S], BF16)
    qhat = gsb("qhat", [128, 2, 4, GT], BF16)
    vau = gsb("vau", [128, 4, NT, 192], BF16)
    onog = gsb("onog", [128, 4, GT], BF16)
    wo_bf = gsb("wo_bf", [128, 4, 1024], BF16)
    pa_sb = gsb("pa_sb", [128, 2, 1024], F32)
    stg_sb = pa_sb[:, 0, :]
    p_sb = [gsb("p0_sb", [128, 1024], BF16), gsb("p1_sb", [128, 1024], BF16),
            gsb("p2_sb", [128, 1024], BF16)]
    msk_sb = gsb("msk_sb", [128, 128], BF16)
    rr_sb = gsb("rr_sb", [32, 2, GT], F32)
    wvr_sb = gsb("wvr_sb", [128, 8, 512], BF16)
    wqr_sb = gsb("wqr_sb", [128, 2, 8, 128], BF16)
    wkr_sb = gsb("wkr_sb", [128, 2, 8, 128], BF16)
    t1_sb = gsb("t1_sb", [128, 2, GT], F32)
    t2_sb = gsb("t2_sb", [128, 2, GT], F32)
    x_sb = gsb("x_sb", [128, 2, 1024], F32)
    ubf_sb = gsb("ubf_sb", [128, 2, 1024], BF16)
    utg_sb = gsb("utg_sb", [128, 8, GT], BF16)
    tq_sb = gsb("tq_sb", [128, GT], F32)
    tk_sb = gsb("tk_sb", [128, GT], F32)
    m1q_sb = gsb("m1q_sb", [128, GT], BF16)
    m2q_sb = gsb("m2q_sb", [128, GT], BF16)
    m1k_sb = gsb("m1k_sb", [128, GT], BF16)
    m2k_sb = gsb("m2k_sb", [128, GT], BF16)
    sqq_sb = gsb("sqq_sb", [128, GT], BF16)
    sqk_sb = gsb("sqk_sb", [128, GT], BF16)
    st_sb = gsb("st_sb", [128, 12], F32)
    ag_sb = gsb("ag_sb", [128, 4], F32)
    sc_sb = gsb("sc_sb", [128, 16], F32)
    wln_sb = gsb("wln_sb", [128, 4], F32)
    r0_sb = gsb("r0_sb", [128, 4], F32)
    g_sb = gsb("g_sb", [128, D], F32)
    bbc_sb = gsb("bbc_sb", [128, D], F32)
    bo_sb = gsb("bo_sb", [128, D], F32)
    x3_sb = gsb("x3_sb", [128, 1024], F32)
    t3_sb = gsb("t3_sb", [128, 1024], F32)
    x13_sb = gsb("x13_sb", [128, 2, 1024], F32)
    zg3_sb = gsb("zg3_sb", [128, 1024], F32)
    st3_sb = gsb("st3_sb", [128, 12], F32)
    ag3_sb = gsb("ag3_sb", [128, 4], F32)
    sc3_sb = gsb("sc3_sb", [128, 16], F32)
    m3_sb = gsb("m3_sb", [128, 2], F32)
    w3_sb = gsb("w3_sb", [128, 2], F32)
    r3_sb = gsb("r3_sb", [128, 2], F32)

    s_pair = [ctx.enter_context(nc.psum_tensor(f"s_pair{i}", [128, 1024], F32)) for i in range(2)]
    o_psA = ctx.enter_context(nc.psum_tensor("o_psA", [128, 512], F32))
    o_psB = ctx.enter_context(nc.psum_tensor("o_psB", [128, 512], F32))
    p1a_ps = ctx.enter_context(nc.psum_tensor("p1a_ps", [128, 512], F32))
    p1b_ps = ctx.enter_context(nc.psum_tensor("p1b_ps", [128, 512], F32))

    block = ctx.enter_context(nc.Block())
    CHUNKS = ([("wq", f) for f in range(8)] + [("wk", f) for f in range(8)]
              + [("wv", f) for f in range(8)] + [("wo", f) for f in range(4)])

    @block.sync
    def _(sync):

        def xdma(tt):
            if tt >= 2:
                sync.wait_ge(sem["s_ln"], tt - 1)
            sync.dma_start(out=x_sb[:, tt % 2, :],
                           in_=x_b[tt * 128:(tt + 1) * 128, :]).then_inc(sem["d_x%d" % (tt % 2)], 16)

        def prep_lite(gp):
            xdma(4 * gp)
            xdma(4 * gp + 1)
            if gp >= 2:
                sync.wait_ge(sem["s_mq"], 4 * (gp - 1))
                sync.wait_ge(sem["s_mk"], 4 * (gp - 1))
            sync.dma_start(out=t1_sb[:, gp % 2, :], in_=t1_d[:, gp * GT:(gp + 1) * GT]).then_inc(sem["d_t%d" % (gp % 2)], 16)
            sync.dma_start(out=t2_sb[:, gp % 2, :], in_=t2_d[:, gp * GT:(gp + 1) * GT]).then_inc(sem["d_t%d" % (gp % 2)], 16)
            xdma(4 * gp + 2)
            xdma(4 * gp + 3)

        def slice_hp(gp, hp):
            idx = 4 * gp + hp
            if idx >= 2:
                sync.wait_ge(sem["s_qmm"], idx - 1)
                sync.wait_ge(sem["s_kmm"], idx - 1)
            if gp == 0:
                sync.wait_ge(sem["d_wb0"], 64 * 4)
                sync.wait_ge(sem["d_wb1"], 64 * 4)
            sync.dma_start(out=wqr_sb[:, hp % 2, :, :],
                           in_=wqh_d.ap()[hp].rearrange("f p c -> p f c")).then_inc(sem["d_wq%d" % (hp % 2)], 16)
            if gp == 0:
                sync.wait_ge(sem["d_wb0"], 64 * 8)
                sync.wait_ge(sem["d_wb1"], 64 * 8)
            sync.dma_start(out=wkr_sb[:, hp % 2, :, :],
                           in_=wkh_d.ap()[hp].rearrange("f p c -> p f c")).then_inc(sem["d_wq%d" % (hp % 2)], 16)

        def slice_dmas(gp):
            for hp in range(4):
                slice_hp(gp, hp)

        def p3load(t3):
            sync.wait_ge(sem["s_cc"], t3 // 2 + 1)
            sync.wait_ge(sem["s_o3"], t3)
            sync.dma_start(out=t3_sb[:],
                           in_=cc_out.ap()[t3 * 128:(t3 + 1) * 128, :]).then_inc(sem["d_a3"], 16)
            sync.dma_start(out=x3_sb[:],
                           in_=x_own[t3 * 128:(t3 + 1) * 128, :]).then_inc(sem["d_a3"], 16)

        def p3out(t3):
            sync.wait_ge(sem["s_o3"], t3 + 1)
            sync.dma_start(out=out_d[t3 * 128:(t3 + 1) * 128, :],
                           in_=zg3_sb[:]).then_inc(sem["d_out"], 16)

        xdma(0)
        xdma(1)
        sync.dma_start(out=t1_sb[:, 0, :], in_=t1_d[:, 0:GT]).then_inc(sem["d_t0"], 16)
        sync.dma_start(out=t2_sb[:, 0, :], in_=t2_d[:, 0:GT]).then_inc(sem["d_t0"], 16)
        for csrc, cdst in [(bq_d, bq_sb[:]), (bk_d, bk_sb[:]), (bv_d, bv_sb[:]),
                           (idn_d, idn_sb[:]), (oblk_d, oblk_sb[:]),
                           (msk_d, msk_sb[:]),
                           (a1_d, a1_sb[:]), (a2_d, a2_sb[:]),
                           (g_d, g_sb[:]), (bb_d, bbc_sb[:]), (bo_d, bo_sb[:])]:
            sync.dma_start(out=cdst, in_=csrc).then_inc(sem["d_wc"], 16)
        def wload(i):
            w, f = CHUNKS[i]
            if i >= 4:
                sync.wait_ge(sem["s_cast"], i - 3)
            wsrc = {"wq": wq_d, "wk": wk_d}[w][f]
            stg4 = [pa_sb[:, 0, 0:512], pa_sb[:, 1, 0:512],
                    x13_sb[:, 0, 0:512], x13_sb[:, 1, 0:512]][i % 4]
            sync.dma_start(out=stg4,
                           in_=wsrc).then_inc(sem["d_ws%d" % (i % 4)], 16)

        for i in range(6):
            wload(i)
        xdma(2)
        for i in range(6, 10):
            wload(i)
        xdma(3)
        for i in range(10, 16):
            wload(i)
        slice_dmas(0)
        for g in range(NG):
            if g >= 2:
                p3load(2 * (g - 2))
            if g < 7:
                prep_lite(g + 1)
                slice_hp(g + 1, 0)
            if g >= 2:
                p3load(2 * (g - 2) + 1)
                p3out(2 * (g - 2))
                p3out(2 * (g - 2) + 1)
            if g < 7:
                slice_hp(g + 1, 1)
                slice_hp(g + 1, 2)
                slice_hp(g + 1, 3)
            if g == 7:
                p3load(12)
                p3load(13)
                p3out(12)
                p3out(13)
        for t3 in (14, 15):
            p3load(t3)
            p3out(t3)
        sync.wait_ge(sem["d_out"], 16 * 16)

    @block.vector
    def _(vector):
        vector = _Draining(vector)

        def cast_chunk(i):
            w, f = CHUNKS[i]
            if i < 16:
                vector.wait_ge(sem["d_ws%d" % (i % 4)], 16 * (i // 4 + 1))
            else:
                vector.wait_ge(sem["d_ws%d" % (4 + i % 4)], 16 * ((i - 16) // 4 + 1))
            stg = [pa_sb[:, 0, :], pa_sb[:, 1, :],
                   x13_sb[:, 0, :], x13_sb[:, 1, :]][i % 4]
            if w in ("wq", "wk"):
                if i >= 2:
                    vector.wait_ge(sem["d_wb%d" % (i % 2)], 64 * (i // 2))
                vector.tensor_copy(p_sb[i % 2][:, 0:512], stg[:, 0:512]).then_inc(sem["s_cast"], 1)
            elif w == "wv":
                vector.tensor_copy(wvr_sb[:, f, :], stg[:, 0:512]).then_inc(sem["s_cast"], 1)
            else:
                vector.tensor_copy(wo_bf[:, f, :], stg[:, 0:1024]).then_inc(sem["s_cast"], 1)

        def ln_a(gp, j):
            tt = 4 * gp + j
            vector.wait_ge(sem["d_x%d" % (tt % 2)], 16 * (tt // 2 + 1))
            if tt >= 2:
                vector.wait_ge(sem["s_trm"], 2 * tt - 2)
            xt = x_sb[:, tt % 2, :]
            vector.bn_stats(st_sb[:, 0:6], xt[:, 0:512])
            vector.bn_stats(st_sb[:, 6:12], xt[:, 512:1024])
            vector.bn_aggr(ag_sb[:, 0:2], st_sb[:, 0:6])
            vector.bn_aggr(ag_sb[:, 2:4], st_sb[:, 6:12])
            m1, v1, m2, v2 = (ag_sb[:, i:i + 1] for i in range(4))
            vector.tensor_add(sc_sb[:, 0:1], m1, m2)
            vector.tensor_scalar_mul(sc_sb[:, 12 + j:13 + j], sc_sb[:, 0:1], 0.5)
            vector.tensor_sub(sc_sb[:, 2:3], m1, m2)
            vector.tensor_mul(sc_sb[:, 3:4], sc_sb[:, 2:3], sc_sb[:, 2:3])
            vector.tensor_add(sc_sb[:, 4:5], v1, v2)
            vector.tensor_scalar_mul(sc_sb[:, 5:6], sc_sb[:, 4:5], 0.5)
            vector.scalar_tensor_tensor(sc_sb[:, 6:7], sc_sb[:, 3:4], 0.25,
                                        sc_sb[:, 5:6], ALU.mult, ALU.add)
            vector.tensor_scalar_add(wln_sb[:, j:j + 1], sc_sb[:, 6:7], 1e-5).then_inc(sem["s_w"], 1)

        def ln_b(gp, j):
            tt = 4 * gp + j
            xt = x_sb[:, tt % 2, :]
            vector.wait_ge(sem["s_r0"], tt + 1)
            r0 = r0_sb[:, j:j + 1]
            vector.tensor_mul(sc_sb[:, 7:8], r0, r0)
            vector.tensor_mul(sc_sb[:, 8:9], sc_sb[:, 7:8], wln_sb[:, j:j + 1])
            vector.tensor_scalar(sc_sb[:, 9:10], sc_sb[:, 8:9], -0.5, 1.5, ALU.mult, ALU.add)
            vector.tensor_mul(sc_sb[:, 10:11], r0, sc_sb[:, 9:10])
            vector.tensor_scalar(ubf_sb[:, tt % 2, :], xt, sc_sb[:, 12 + j:13 + j],
                                 sc_sb[:, 10:11], ALU.subtract, ALU.mult).then_inc(sem["s_ln"], 1)

        def onorm_block(g, hp):
            vector.wait_ge(sem["s_rr"], 4 * g + hp + 1)
            vector.tensor_mul(onog[0:32, hp, :], o_psA[0:32, :], rr_sb[:, 0, :])
            vector.tensor_mul(onog[32:64, hp, :], o_psA[32:64, :], rr_sb[:, 0, :])
            vector.tensor_mul(onog[64:96, hp, :], o_psB[0:32, :], rr_sb[:, 1, :])
            vector.tensor_mul(onog[96:128, hp, :], o_psB[32:64, :], rr_sb[:, 1, :]).then_inc(sem["s_onorm"], 1)

        def dve_stages(gp):
            st = []
            for j in range(4):
                st.append((("lna", j), lambda j=j: ln_a(gp, j)))
                st.append((("lnb", j), lambda j=j: ln_b(gp, j)))
            for j in range(4):
                tt = 4 * gp + j

                def eva(j=j, tt=tt):
                    if gp >= 1:
                        vector.wait_ge(sem["s_vmm"], 4 * (gp - 1) + j + 1)
                    vector.wait_ge(sem["s_trm"], 2 * tt + 1)
                    vector.tensor_copy(utg_sb[:, 0:4, j * 128:(j + 1) * 128],
                                       p1a_ps[:].rearrange("p (c k) -> p c k", c=4)).then_inc(sem["s_utg"], 1)

                def evb(j=j, tt=tt):
                    vector.wait_ge(sem["s_trm"], 2 * tt + 2)
                    vector.tensor_copy(utg_sb[:, 4:8, j * 128:(j + 1) * 128],
                                       p1b_ps[:].rearrange("p (c k) -> p c k", c=4)).then_inc(sem["s_utg"], 1)

                st.append((("eva", j), eva))
                st.append((("evb", j), evb))
            t1g, t2g = t1_sb[:, gp % 2, :], t2_sb[:, gp % 2, :]
            for hp in range(4):
                idx = 4 * gp + hp

                def mq(hp=hp, idx=idx):
                    vector.wait_ge(sem["s_qmm"], idx + 1)
                    vector.wait_ge(sem["s_cq"], idx)
                    vector.wait_ge(sem["d_t%d" % (gp % 2)], 32 * (gp // 2 + 1))
                    vector.tensor_scalar_add(tq_sb[:], p1a_ps[:], bq_sb[:, hp:hp + 1])
                    vector.scalar_tensor_tensor(m1q_sb[:], tq_sb[:], 0.125, t1g, ALU.mult, ALU.mult)
                    vector.scalar_tensor_tensor(m2q_sb[:], tq_sb[:], 0.125, t2g, ALU.mult, ALU.mult).then_inc(sem["s_mq"], 1)

                def mk(hp=hp, idx=idx):
                    vector.wait_ge(sem["s_kmm"], idx + 1)
                    vector.wait_ge(sem["s_ck"], idx)
                    vector.tensor_scalar_add(tk_sb[:], p1b_ps[:], bk_sb[:, hp:hp + 1])
                    vector.tensor_mul(m1k_sb[:], tk_sb[:], t1g)
                    vector.tensor_mul(m2k_sb[:], tk_sb[:], t2g).then_inc(sem["s_mk"], 1)

                def sqq(hp=hp, idx=idx):
                    vector.wait_ge(sem["s_cq"], idx + 1)
                    vector.tensor_copy(qhat[:, gp % 2, hp, :], p1a_ps[:]).then_inc(sem["s_qhe"], 1)
                    vector.tensor_mul(sqq_sb[:], qhat[:, gp % 2, hp, :],
                                      qhat[:, gp % 2, hp, :]).then_inc(sem["s_rq"], 1)

                def sqk(hp=hp, idx=idx):
                    vector.wait_ge(sem["s_ck"], idx + 1)
                    if idx >= 1:
                        vector.wait_ge(sem["s_nmm"], 2 * idx)
                    vector.tensor_copy(khat[:, hp, gp * GT:(gp + 1) * GT],
                                       p1b_ps[:]).then_inc(sem["s_khe"], 1)
                    vector.tensor_mul(sqk_sb[:], khat[:, hp, gp * GT:(gp + 1) * GT],
                                      khat[:, hp, gp * GT:(gp + 1) * GT]).then_inc(sem["s_rk"], 1)

                def rmq(hp=hp, idx=idx):
                    vector.wait_ge(sem["s_nmm"], 2 * idx + 1)
                    vector.reduce_max(nmaxq_sb[:, hp:hp + 1], p1a_ps[0:64, :], AXX).then_inc(sem["s_nmax"], 1)

                def rmk(hp=hp, idx=idx):
                    vector.wait_ge(sem["s_nmm"], 2 * idx + 2)
                    vector.reduce_max(nmaxk_sb[:, hp:hp + 1], p1b_ps[0:64, :], AXX).then_inc(sem["s_nmax"], 1)

                for key, fn in (("mq", mq), ("mk", mk), ("sqq", sqq),
                                ("sqk", sqk), ("rmq", rmq), ("rmk", rmk)):
                    st.append(((key, hp), fn))

            def b2():
                if gp == 0:
                    vector.tensor_copy(nk2_sb[:], nmaxk_sb[:])
                else:
                    vector.tensor_max(nk2_sb[:], nk2_sb[:], nmaxk_sb[:])
                vector.tensor_mul(bsc_sb[:, 0:4], nmaxq_sb[:], nk2_sb[:]).then_inc(sem["s_b2"], 1)

            def bfin():
                vector.wait_ge(sem["s_bexp"], gp + 1)
                vector.tensor_scalar_mul(bsc_sb[:, 8:12], bsc_sb[:, 4:8], -1.0)
                vector.tensor_copy(p1b_ps[32:64, 0:4], bsc_sb[32:64, 8:12])
                vector.tensor_tensor(bias_sb[0:32, gp % 2, :], bsc_sb[0:32, 8:12],
                                     p1b_ps[32:64, 0:4], ALU.min)
                vector.tensor_copy(bias_sb[32:64, gp % 2, :], bias_sb[0:32, gp % 2, :])
                vector.tensor_copy(bias_sb[64:128, gp % 2, :], bias_sb[0:64, gp % 2, :]).then_inc(sem["s_bias"], 1)

            st.append((("b2", 0), b2))
            st.append((("bfin", 0), bfin))
            for j in range(4):
                tt = 4 * gp + j

                def vev(j=j, tt=tt):
                    vector.wait_ge(sem["s_vmm"], tt + 1)
                    if gp == 0:
                        vp = [o_psA[:, 0:512], o_psB[:, 0:512],
                              s_pair[0][:, 0:512], s_pair[1][:, 0:512]][j]
                    else:
                        vp = (p1a_ps if j % 2 == 0 else p1b_ps)[:]
                    vector.tensor_add(
                        vau[:, :, tt, :].rearrange("p h (t c) -> p h t c", t=2)[:, :, :, 0:64],
                        vp.rearrange("p (h t c) -> p h t c", h=4, t=2),
                        bv_sb[:].rearrange("p (h t c) -> p h t c", h=4, t=2),
                    ).then_inc(sem["s_vev"], 1)

                st.append((("vev", j), vev))
            return st

        def pa_evac(g, j):
            tt = 4 * g + j
            vector.wait_ge(sem["s_womm"], tt + 1)
            if tt >= 2:
                vector.wait_ge(sem["d_pa%d" % (tt % 2)], 16 * (tt // 2))
            vector.tensor_copy(pa_sb[:, tt % 2, 0:512], p1a_ps[:])
            vector.tensor_copy(pa_sb[:, tt % 2, 512:1024], p1b_ps[:]).then_inc(sem["s_paev"], 1)

        def p3a(t3):
            vector.wait_ge(sem["d_a3"], 32 * (t3 + 1))
            vector.tensor_add(t3_sb[:], t3_sb[:], x3_sb[:]).then_inc(sem["s_t3"], 1)
            x1 = x13_sb[:, t3 % 2, :]
            vector.tensor_add(x1, t3_sb[:], bo_sb[:])
            vector.bn_stats(st3_sb[:, 0:6], x1[:, 0:512])
            vector.bn_stats(st3_sb[:, 6:12], x1[:, 512:1024])
            vector.bn_aggr(ag3_sb[:, 0:2], st3_sb[:, 0:6])
            vector.bn_aggr(ag3_sb[:, 2:4], st3_sb[:, 6:12])
            m1, v1, m2, v2 = (ag3_sb[:, i:i + 1] for i in range(4))
            vector.tensor_add(sc3_sb[:, 0:1], m1, m2)
            vector.tensor_scalar_mul(m3_sb[:, t3 % 2:t3 % 2 + 1], sc3_sb[:, 0:1], 0.5)
            vector.tensor_sub(sc3_sb[:, 2:3], m1, m2)
            vector.tensor_mul(sc3_sb[:, 3:4], sc3_sb[:, 2:3], sc3_sb[:, 2:3])
            vector.tensor_add(sc3_sb[:, 4:5], v1, v2)
            vector.tensor_scalar_mul(sc3_sb[:, 5:6], sc3_sb[:, 4:5], 0.5)
            vector.scalar_tensor_tensor(sc3_sb[:, 6:7], sc3_sb[:, 3:4], 0.25,
                                        sc3_sb[:, 5:6], ALU.mult, ALU.add)
            vector.tensor_scalar(w3_sb[:, t3 % 2:t3 % 2 + 1], sc3_sb[:, 6:7],
                                 1.00001, 1e-10, ALU.mult, ALU.add).then_inc(sem["s_w3"], 1)

        def p3b(t3):
            x1 = x13_sb[:, t3 % 2, :]
            vector.wait_ge(sem["s_r3"], t3 + 1)
            r0 = r3_sb[:, t3 % 2:t3 % 2 + 1]
            vector.tensor_mul(sc3_sb[:, 7:8], r0, r0)
            vector.tensor_mul(sc3_sb[:, 8:9], sc3_sb[:, 7:8], w3_sb[:, t3 % 2:t3 % 2 + 1])
            vector.tensor_scalar(sc3_sb[:, 9:10], sc3_sb[:, 8:9], -0.5, 1.5, ALU.mult, ALU.add)
            vector.tensor_mul(sc3_sb[:, 10:11], r0, sc3_sb[:, 9:10])
            if t3 >= 1:
                vector.wait_ge(sem["d_out"], 16 * t3)
            vector.tensor_scalar(zg3_sb[:], x1, m3_sb[:, t3 % 2:t3 % 2 + 1],
                                 sc3_sb[:, 10:11], ALU.subtract, ALU.mult)
            vector.tensor_mul(zg3_sb[:], zg3_sb[:], g_sb[:])
            vector.tensor_add(t3_sb[:], x1, bbc_sb[:])
            vector.tensor_add(zg3_sb[:], zg3_sb[:], t3_sb[:]).then_inc(sem["s_o3"], 1)

        st0 = dve_stages(0)
        st0k = sorted(enumerate(st0), key=lambda t: ((min(_sched(0)[t[1][0]], 16), t[0])
                                                     if t[1][0][0] not in ("v", "vev")
                                                     else (15, 1000 + t[1][0][1])))
        early = [fn for _i, (key, fn) in st0k if key[0] in ("lna", "lnb", "eva", "evb")]
        late = [fn for _i, (key, fn) in st0k if key[0] not in ("lna", "lnb", "eva", "evb")]
        cast_chunk(0)
        cast_chunk(1)
        vector.wait_ge(sem["d_wc"], 16 * NCONST)
        vector.memset(vau[:, :, :, 64:96], 1.0)
        vector.memset(vau[:, :, :, 160:192], 1.0).then_inc(sem["s_onorm"], 1)
        for i in range(2, 28):
            cast_chunk(i)
            if i - 2 < len(early):
                early[i - 2]()
        for _fn in late:
            _fn()
        for g in range(NG):
            n = 16 * (g + 1)
            L = 4 * g + 4
            items = []
            if g < 7:
                pos = _sched(g)
                for seq, (key, fn) in enumerate(dve_stages(g + 1)):
                    items.append((pos[key], 1, seq, fn))
            for hp in range(4):
                items.append((min((hp + 1) * L, n), 2, hp,
                              lambda g=g, hp=hp: onorm_block(g, hp)))
            if g >= 1:
                for j in range(4):
                    items.append((2 * j + 2, 3, j, lambda g=g, j=j: pa_evac(g - 1, j)))
            if g >= 2:
                t3 = 2 * (g - 2)
                items.append((3, 1, 1000, lambda t3=t3: p3a(t3)))
                items.append((8, 1, 1001, lambda t3=t3: p3b(t3)))
                items.append((13, 1, 1002, lambda t3=t3: p3a(t3 + 1)))
                items.append((18, 1, 1003, lambda t3=t3: p3b(t3 + 1)))
            if g == 7:
                items.append((40, 1, 1010, lambda: p3a(12)))
                items.append((45, 1, 1011, lambda: p3b(12)))
                items.append((50, 1, 1012, lambda: p3a(13)))
                items.append((55, 1, 1013, lambda: p3b(13)))
            items.sort(key=lambda x: (x[0], x[1], x[2]))
            for _p, _q, _s, fn in items:
                fn()
        for j in range(4):
            pa_evac(7, j)
        for t3 in (14, 15):
            p3a(t3)
            p3b(t3)

    @block.scalar
    def _(scalar):
        scalar = _Draining(scalar)
        scalar.wait_ge(sem["d_wc"], 16 * NCONST)

        def r0_tile(gp, j):
            tt = 4 * gp + j
            scalar.wait_ge(sem["s_w"], tt + 1)
            scalar.activation(r0_sb[:, j:j + 1], wln_sb[:, j:j + 1], AF.Ln)
            scalar.activation(r0_sb[:, j:j + 1], r0_sb[:, j:j + 1], AF.Exp, scale=-0.5).then_inc(sem["s_r0"], 1)

        def act_stages(gp):
            st = []
            for j in range(4):
                st.append((("r0", j), lambda j=j: r0_tile(gp, j)))

            def bexp():
                scalar.wait_ge(sem["s_b2"], gp + 1)
                scalar.activation(bsc_sb[:, 4:8], bsc_sb[:, 0:4], AF.Ln)
                scalar.activation(bsc_sb[:, 4:8], bsc_sb[:, 4:8], AF.Exp, scale=0.5).then_inc(sem["s_bexp"], 1)

            st.append((("bexp", 0), bexp))
            return st

        def p3_act(t3):
            scalar.wait_ge(sem["s_w3"], t3 + 1)
            scalar.activation(r3_sb[:, t3 % 2:t3 % 2 + 1], w3_sb[:, t3 % 2:t3 % 2 + 1], AF.Ln)
            scalar.activation(r3_sb[:, t3 % 2:t3 % 2 + 1], r3_sb[:, t3 % 2:t3 % 2 + 1],
                              AF.Exp, scale=-0.5).then_inc(sem["s_r3"], 1)

        for _fn in _chain_order(act_stages(0)):
            _fn()
        for g in range(NG):
            n = 16 * (g + 1)
            items = []
            if g < 7:
                pos = _sched(g)
                for seq, (key, fn) in enumerate(act_stages(g + 1)):
                    items.append((pos[key], seq, fn))
            if g >= 2:
                t3 = 2 * (g - 2)
                items.append((6, 1000, lambda t3=t3: p3_act(t3)))
                items.append((16, 1001, lambda t3=t3: p3_act(t3 + 1)))
            if g == 7:
                items.append((43, 1010, lambda: p3_act(12)))
                items.append((53, 1011, lambda: p3_act(13)))
            items.sort(key=lambda x: (x[0], x[1]))
            si = 0
            t = 0
            for i in range(UOFF[g], UOFF[g + 1]):
                gg, hp, kt = UNITS[i]
                scalar.wait_ge(sem["s_qk2"], i + 1)
                if i >= 3:
                    scalar.wait_ge(sem["s_pv"], i - 2)
                if kt == 0 and hp == 0:
                    scalar.wait_ge(sem["s_bias"], g + 1)
                dq0 = 128 * (kt - 4 * g) if kt >= 4 * g else 0
                if dq0 > 0:
                    esrc = s_pair[i % 2][:].rearrange("p (h q) -> p h q", h=2)[:, :, dq0:512]
                    edst = p_sb[i % 3][:].rearrange("p (h q) -> p h q", h=2)[:, :, dq0:512]
                else:
                    esrc, edst = s_pair[i % 2][:], p_sb[i % 3][:]
                scalar.activation(edst, esrc, AF.Exp,
                                  bias=bias_sb[:, g % 2, hp:hp + 1]).then_inc(sem["s_exp"], 1)
                if kt == 4 * g + 3:
                    scalar.wait_ge(sem["s_pv"], ULAST[(g, hp)] + 1)
                    scalar.activation(rr_sb[:, 0, :], o_psA[64:96, :], AF.Ln)
                    scalar.activation(rr_sb[:, 0, :], rr_sb[:, 0, :], AF.Exp, scale=-1.0)
                    scalar.activation(rr_sb[:, 1, :], o_psB[64:96, :], AF.Ln)
                    scalar.activation(rr_sb[:, 1, :], rr_sb[:, 1, :], AF.Exp, scale=-1.0).then_inc(sem["s_rr"], 1)
                while si < len(items) and items[si][0] <= t:
                    items[si][2]()
                    si += 1
                t += 1
            while si < len(items):
                items[si][2]()
                si += 1
        for t3 in (14, 15):
            p3_act(t3)

    @block.gpsimd
    def _(gpsimd):
        gpsimd = _Draining(gpsimd)
        def wchunk_wb(i):
            w, f = CHUNKS[i]
            gpsimd.wait_ge(sem["s_cast"], i + 1)
            dstd = wqh_d if w == "wq" else wkh_d
            for hp in range(4):
                gpsimd.dma_start(out=dstd.ap()[hp, f, :, :],
                                 in_=p_sb[i % 2][:, hp * 128:(hp + 1) * 128]).then_inc(sem["d_wb%d" % (i % 2)], 16)

        def wload2(i):
            w, f = CHUNKS[i]
            gpsimd.wait_ge(sem["s_cast"], i - 3)
            wsrc = {"wv": wv_d, "wo": wo_d}[w][f]
            width = 1024 if w == "wo" else 512
            stg4 = [pa_sb[:, 0, :], pa_sb[:, 1, :],
                    x13_sb[:, 0, :], x13_sb[:, 1, :]][i % 4]
            gpsimd.dma_start(out=stg4[:, 0:width],
                             in_=wsrc).then_inc(sem["d_ws%d" % (4 + i % 4)], 16)

        for j in range(14):
            wchunk_wb(j)
        wload2(16)
        wchunk_wb(14)
        wload2(17)
        wchunk_wb(15)
        for i in range(18, 28):
            wload2(i)
        gpsimd.wait_ge(sem["d_wc"], 16 * NCONST)

        for g in range(NG):
            for j in range(4):
                tt = 4 * g + j
                gpsimd.wait_ge(sem["s_paev"], tt + 1)
                gpsimd.dma_start(out=cc_in.ap()[tt * 128:(tt + 1) * 128, :],
                                 in_=pa_sb[:, tt % 2, :]).then_inc(sem["d_pa%d" % (tt % 2)], 16)
            gpsimd.wait_ge(sem["d_pa0"], 16 * 2 * (g + 1))
            gpsimd.wait_ge(sem["d_pa1"], 16 * 2 * (g + 1))
            gpsimd.collective_compute(
                "ReduceScatter", ALU.add,
                replica_groups=[[0, 1], [2, 3], [4, 5], [6, 7]],
                ins=[cc_in.ap()[g * GT:(g + 1) * GT, :].opt()],
                outs=[cc_out.ap()[g * 256:(g + 1) * 256, :].opt()],
            ).then_inc(sem["s_cc"], 1)

    @block.tensor
    def _(tensor):
        tensor.wait_ge(sem["d_wc"], 16 * NCONST)

        def pe_stages(gp):
            st = []
            for j in range(4):
                tt = 4 * gp + j

                def tma(j=j, tt=tt):
                    tensor.wait_ge(sem["s_ln"], tt + 1)
                    if j == 0:
                        if gp >= 2:
                            tensor.wait_ge(sem["s_paev"], 4 * (gp - 2) + 4)
                        if gp >= 1:
                            tensor.wait_ge(sem["s_vev"], 4 * (gp - 1) + 3)
                    else:
                        tensor.wait_ge(sem["s_utg"], 2 * (tt - 1) + 1)
                    for c in range(4):
                        ins = tensor.matmul(p1a_ps[:, c * 128:(c + 1) * 128],
                                            ubf_sb[:, tt % 2, c * 128:(c + 1) * 128],
                                            idn_sb[:], start=True, stop=True)
                    ins.then_inc(sem["s_trm"], 1)

                def tmb(j=j, tt=tt):
                    if j == 0:
                        if gp >= 2:
                            tensor.wait_ge(sem["s_paev"], 4 * (gp - 2) + 4)
                        if gp >= 1:
                            tensor.wait_ge(sem["s_vev"], 4 * (gp - 1) + 4)
                    else:
                        tensor.wait_ge(sem["s_utg"], 2 * (tt - 1) + 2)
                    for c in range(4):
                        ins = tensor.matmul(p1b_ps[:, c * 128:(c + 1) * 128],
                                            ubf_sb[:, tt % 2, (c + 4) * 128:(c + 5) * 128],
                                            idn_sb[:], start=True, stop=True)
                    ins.then_inc(sem["s_trm"], 1)

                st.append((("tma", j), tma))
                st.append((("tmb", j), tmb))
            for hp in range(4):
                idx = 4 * gp + hp
                sidx = 2 * gp + hp // 2

                def qmm(hp=hp, idx=idx, sidx=sidx):
                    if hp == 0:
                        tensor.wait_ge(sem["s_utg"], 8 * (gp + 1))
                    tensor.wait_ge(sem["d_wq%d" % (hp % 2)], 32 * (sidx + 1))
                    if idx >= 1:
                        tensor.wait_ge(sem["s_nmax"], 2 * (idx - 1) + 1)
                    for f in range(8):
                        ins = tensor.matmul(p1a_ps[:], wqr_sb[:, hp % 2, f, :],
                                            utg_sb[:, f, :], start=(f == 0), stop=(f == 7))
                    ins.then_inc(sem["s_qmm"], 1)

                def kmm(hp=hp, idx=idx):
                    if idx >= 1:
                        tensor.wait_ge(sem["s_nmax"], 2 * (idx - 1) + 2)
                    for f in range(8):
                        ins = tensor.matmul(p1b_ps[:], wkr_sb[:, hp % 2, f, :],
                                            utg_sb[:, f, :], start=(f == 0), stop=(f == 7))
                    ins.then_inc(sem["s_kmm"], 1)

                def cmbq(hp=hp, idx=idx):
                    tensor.wait_ge(sem["s_mq"], idx + 1)
                    tensor.matmul(p1a_ps[:], a1_sb[:], m1q_sb[:], start=True, stop=False)
                    tensor.matmul(p1a_ps[:], a2_sb[:], m2q_sb[:], start=False, stop=True).then_inc(sem["s_cq"], 1)

                def cmbk(hp=hp, idx=idx):
                    tensor.wait_ge(sem["s_mk"], idx + 1)
                    tensor.matmul(p1b_ps[:], a1_sb[:], m1k_sb[:], start=True, stop=False)
                    tensor.matmul(p1b_ps[:], a2_sb[:], m2k_sb[:], start=False, stop=True).then_inc(sem["s_ck"], 1)

                def nq(hp=hp, idx=idx):
                    tensor.wait_ge(sem["s_qhe"], idx + 1)
                    tensor.wait_ge(sem["s_rq"], idx + 1)
                    tensor.matmul(p1a_ps[0:64, :], oblk_sb[:, 0:64], sqq_sb[:],
                                  start=True, stop=True).then_inc(sem["s_nmm"], 1)

                def nk(hp=hp, idx=idx):
                    tensor.wait_ge(sem["s_khe"], idx + 1)
                    tensor.wait_ge(sem["s_rk"], idx + 1)
                    tensor.matmul(p1b_ps[0:64, :], oblk_sb[:, 0:64], sqk_sb[:],
                                  start=True, stop=True).then_inc(sem["s_nmm"], 1)

                for key, fn in (("qmm", qmm), ("kmm", kmm), ("cmbq", cmbq),
                                ("cmbk", cmbk), ("nq", nq), ("nk", nk)):
                    st.append(((key, hp), fn))
            for j in range(4):
                tt = 4 * gp + j

                def vmm(j=j, tt=tt):
                    if gp == 0:
                        vp = [o_psA[:, 0:512], o_psB[:, 0:512],
                              s_pair[0][:, 0:512], s_pair[1][:, 0:512]][j]
                        if j == 0:
                            tensor.wait_ge(sem["s_cast"], 24)
                            tensor.wait_ge(sem["s_utg"], 8)
                    else:
                        vp = (p1a_ps if j % 2 == 0 else p1b_ps)[:]
                        if j < 2:
                            tensor.wait_ge(sem["s_nmax"], 8 * gp + 7 + j)
                            if j == 1:
                                tensor.wait_ge(sem["s_bias"], gp + 1)
                        else:
                            tensor.wait_ge(sem["s_vev"], tt - 1)
                    for f in range(8):
                        ins = tensor.matmul(vp, utg_sb[:, f, j * 128:(j + 1) * 128],
                                            wvr_sb[:, f, :], start=(f == 0), stop=(f == 7))
                    ins.then_inc(sem["s_vmm"], 1)

                st.append((("v", j), vmm))
            return st

        def emit_pv(u):
            g, hp, kt = UNITS[u]
            first, last = (kt == 0), (kt == 4 * g + 3)
            diag = kt >= 4 * g
            q0 = 128 * (kt - 4 * g) if diag else 0
            pa, pb = p_sb[u % 3][:, q0:512], p_sb[u % 3][:, 512 + q0:1024]
            tensor.wait_ge(sem["s_exp"], u + 1)
            if diag:
                tensor.wait_ge(sem["s_vev"], kt + 1)
            if first:
                tensor.wait_ge(sem["s_onorm"], 4 * g + hp + 1)
            tensor.matmul(o_psA[0:96, q0:512], vau[:, hp, kt, 0:96], pa,
                          start=first, stop=last, skip_group_check=True)
            tensor.matmul(o_psB[0:96, q0:512], vau[:, hp, kt, 96:192], pb,
                          start=first, stop=last, skip_group_check=True).then_inc(sem["s_pv"], 1)

        def wo_stage(gm, j):
            tt = 4 * gm + j
            if gm == 0 and j == 0:
                tensor.wait_ge(sem["s_cast"], NWC)
            if j == 0:
                tensor.wait_ge(sem["s_onorm"], 4 * (gm + 1) + 1)
                tensor.wait_ge(sem["s_vev"], min(4 * (gm + 2), 32))
            else:
                tensor.wait_ge(sem["s_paev"], tt)
            for half in range(2):
                wo_ps = p1a_ps if half == 0 else p1b_ps
                for hp in range(4):
                    ins = tensor.matmul(wo_ps[:],
                                        onog[:, hp, j * 128:(j + 1) * 128],
                                        wo_bf[:, hp, half * 512:(half + 1) * 512],
                                        start=(hp == 0), stop=(hp == 3), skip_group_check=True)
            ins.then_inc(sem["s_womm"], 1)

        for _fn in _chain_order(pe_stages(0)):
            _fn()
        for g in range(NG):
            items = []
            if g < 7:
                pos = _sched(g)
                for seq, (key, fn) in enumerate(pe_stages(g + 1)):
                    items.append((pos[key], seq, fn))
            if g >= 1:
                for j in range(4):
                    items.append((2 * j + 1, -100 + j, lambda g=g, j=j: wo_stage(g - 1, j)))
            items.sort(key=lambda x: (x[0], x[1]))
            si = 0
            t = 0
            gc2 = (g % 2)
            pend = None
            u = UOFF[g]
            for hp in range(4):
                for kt in range(4 * g + 4):
                    if u >= 2:
                        tensor.wait_ge(sem["s_exp"], u - 1)
                    if kt == 0:
                        tensor.wait_ge(sem["s_qhe"], 4 * g + hp + 1)
                        tensor.wait_ge(sem["s_khe"], 4 * g + hp + 1)
                    if g == 0 and u < 2:
                        tensor.wait_ge(sem["s_vev"], 3 + u)
                    kc = slice(kt * 128, (kt + 1) * 128)
                    diag = kt >= 4 * g
                    q0 = 128 * (kt - 4 * g) if diag else 0
                    tensor.matmul(s_pair[u % 2][:, q0:512], khat[0:64, hp, kc],
                                  qhat[0:64, gc2, hp, q0:512], start=True, stop=not diag,
                                  skip_group_check=True)
                    ins2 = tensor.matmul(s_pair[u % 2][:, 512 + q0:1024], khat[64:128, hp, kc],
                                         qhat[64:128, gc2, hp, q0:512], start=True, stop=not diag,
                                         skip_group_check=True)
                    if diag:
                        tensor.matmul(s_pair[u % 2][:, q0:q0 + 128], idn_sb[:],
                                      msk_sb[:], start=False, stop=True, skip_group_check=True)
                        tensor.matmul(s_pair[u % 2][:, 512 + q0:512 + q0 + 128], idn_sb[:],
                                      msk_sb[:], start=False, stop=True,
                                      skip_group_check=True).then_inc(sem["s_qk2"], 1)
                    else:
                        ins2.then_inc(sem["s_qk2"], 1)
                    if pend is not None:
                        emit_pv(pend)
                    pend = u
                    u += 1
                    while si < len(items) and items[si][0] <= t:
                        items[si][2]()
                        si += 1
                    t += 1
            emit_pv(pend)
            while si < len(items):
                items[si][2]()
                si += 1
        for j in range(4):
            wo_stage(7, j)

    return nc, ctx


_PROG = None


def _rope_tables():
    inv_freq = (1.0 / 10000.0 ** (np.arange(0, DK, 2, dtype=np.float32) / np.float32(DK))).astype(np.float32)
    pos = np.arange(S, dtype=np.float32)
    fr = (pos[:, None] * inv_freq[None, :]).astype(np.float32)
    c = np.cos(fr.astype(np.float64)).astype(np.float32).T
    s = np.sin(fr.astype(np.float64)).astype(np.float32).T
    t1 = np.ascontiguousarray(np.concatenate([c, c, -s, -s], axis=0))
    t2 = np.ascontiguousarray(np.concatenate([s, s, c, c], axis=0))
    return t1, t2


def kernel(**inputs):
    global _PROG
    if _PROG is None:
        _PROG = _build()
    nc, _ctx = _PROG

    x = np.asarray(inputs["L0_attn_x"], np.float32)
    Wq = np.asarray(inputs["L0_attn_Wq"], np.float32)
    Wk = np.asarray(inputs["L0_attn_Wk"], np.float32)
    Wv = np.asarray(inputs["L0_attn_Wv"], np.float32)
    Wo = np.asarray(inputs["L0_attn_Wo"], np.float32)
    bq = np.asarray(inputs["L0_attn_bq"], np.float32)
    bk = np.asarray(inputs["L0_attn_bk"], np.float32)
    bv = np.asarray(inputs["L0_attn_bv"], np.float32)
    bo = np.asarray(inputs["L0_attn_bo"], np.float32)
    ln_g = np.asarray(inputs["L0_ffn_ln_g"], np.float32)
    ln_b = np.asarray(inputs["L0_ffn_ln_b"], np.float32)

    t1, t2 = _rope_tables()
    kk = np.arange(128)[:, None]
    qq = np.arange(GT)[None, :]
    kk128 = np.arange(128)[:, None]
    qq128 = np.arange(128)[None, :]
    msk = np.where(kk128 <= qq128, np.float32(0), np.float32(-1e9)).astype(BF)
    idn = np.eye(128, dtype=np.float32).astype(BF)
    oblk = np.zeros((128, 64), np.float32)
    oblk[0:64, 0:32] = 1.0
    oblk[64:128, 32:64] = 1.0
    oblk = oblk.astype(BF)
    map1 = np.concatenate([np.arange(32), np.arange(32, 64) + 32,
                           np.arange(64, 96) - 64, np.arange(96, 128) - 32])
    map2 = np.concatenate([np.arange(32) + 32, np.arange(32, 64) + 64,
                           np.arange(64, 96) - 32, np.arange(96, 128)])
    a1 = np.zeros((128, 128), np.float32)
    a1[np.arange(128), map1] = 1.0
    a2 = np.zeros((128, 128), np.float32)
    a2[np.arange(128), map2] = 1.0
    a1 = a1.astype(BF)
    a2 = a2.astype(BF)
    g_bc = np.ascontiguousarray(np.broadcast_to(ln_g, (128, D)), dtype=np.float32)
    b_bc = np.ascontiguousarray(np.broadcast_to(ln_b, (128, D)), dtype=np.float32)
    bo_bc = np.ascontiguousarray(np.broadcast_to(bo, (128, D)), dtype=np.float32)

    in_maps = []
    for c in range(8):
        b, hg = c // 2, c % 2
        heads = [hg * 8 + i for i in range(8)]
        perm = []
        for hp in range(4):
            a, b2 = heads[2 * hp], heads[2 * hp + 1]
            perm += (list(a * 64 + np.arange(0, 64, 2)) + list(b2 * 64 + np.arange(0, 64, 2))
                     + list(a * 64 + np.arange(1, 64, 2)) + list(b2 * 64 + np.arange(1, 64, 2)))
        perm = np.asarray(perm)
        vcols = np.arange(hg * 512, hg * 512 + 512)
        worows = []
        for hp in range(4):
            a, b2 = heads[2 * hp], heads[2 * hp + 1]
            worows += list(a * 64 + np.arange(64)) + list(b2 * 64 + np.arange(64))
        worows = np.asarray(worows)
        own_rows = np.concatenate([512 * gg + 256 * hg + np.arange(256) for gg in range(8)])
        in_maps.append({
            "x_b": np.ascontiguousarray(x[b]),
            "x_own": np.ascontiguousarray(x[b][own_rows]),
            "wq_d": np.ascontiguousarray(Wq[:, perm]).reshape(8, 128, 512),
            "wk_d": np.ascontiguousarray(Wk[:, perm]).reshape(8, 128, 512),
            "wv_d": np.ascontiguousarray(Wv[:, vcols]).reshape(8, 128, 512),
            "wo_d": np.ascontiguousarray(Wo[worows, :]).reshape(4, 128, 1024),
            "bq_d": np.ascontiguousarray(bq[perm].reshape(4, 128).T),
            "bk_d": np.ascontiguousarray(bk[perm].reshape(4, 128).T),
            "bv_d": np.ascontiguousarray(np.broadcast_to(bv[vcols], (128, 512)), dtype=np.float32),
            "t1_d": t1, "t2_d": t2, "msk_d": np.ascontiguousarray(msk),
            "idn_d": idn, "oblk_d": oblk,
            "a1_d": a1, "a2_d": a2,
            "g_d": g_bc, "bb_d": b_bc, "bo_d": bo_bc,
        })

    res = run_bass_kernel_spmd(nc, in_maps, core_ids=list(range(8)))
    out = np.empty((B, S, D), np.float32)
    for c in range(8):
        b, hg = c // 2, c % 2
        o = np.asarray(res.results[c]["out"]).reshape(8, 256, D)
        for gg in range(8):
            out[b, 512 * gg + 256 * hg: 512 * gg + 256 * (hg + 1)] = o[gg]
    return out
